# version 12
# baseline (speedup 1.0000x reference)
import bisect
import contextlib
import numpy as np
import concourse.bass as bass
import concourse.mybir as mybir
from concourse.bass_utils import run_bass_kernel_spmd

F32 = mybir.dt.float32
BF16 = mybir.dt.bfloat16
AF = mybir.ActivationFunctionType
ALU = mybir.AluOpType
AX = mybir.AxisListType.X

NCORE = 8
TLOC = 2048
C = 2048
NCH = 16
NHEAD = 16
NEXP = 8
NF = 11
NFS = 22
HALF = 1024
E10 = float(np.float32(np.exp(10.0)))
EM10 = float(np.float32(np.exp(-10.0)))


class Prog:
    def __init__(self):
        self.ops = []
        self.lastw = {}
        self.readers = {}
        self.dma_cnt = {}

    def op(self, eng, emit, reads=(), writes=(), dma=None, tag="", extra_deps=(),
           small=False):
        oid = len(self.ops)
        deps = set(extra_deps)
        for k in reads:
            w = self.lastw.get(k)
            if w is not None:
                deps.add(w)
            self.readers.setdefault(k, []).append(oid)
        for k in writes:
            for r in self.readers.get(k, ()):
                deps.add(r)
            w = self.lastw.get(k)
            if w is not None:
                deps.add(w)
            self.readers[k] = []
            self.lastw[k] = oid
        deps.discard(oid)
        d = None
        if dma is not None:
            name, inc, mode = dma
            idx = self.dma_cnt.get(name, 0) + 1
            self.dma_cnt[name] = idx
            d = (name, inc, mode, idx)
        self.ops.append({"id": oid, "eng": eng, "emit": emit, "deps": deps,
                         "dma": d, "inc": False, "ord": None, "tag": tag,
                         "small": small})
        return oid

    def join(self):
        lasts = {}
        for o in self.ops:
            if o["emit"] is not None:
                lasts[o["eng"]] = o["id"]
        dma_marks = dict(self.dma_cnt)
        for eng in ("t", "v", "s", "y", "g"):
            oid = len(self.ops)
            deps = set(v for k, v in lasts.items() if k != eng)
            self.ops.append({"id": oid, "eng": eng, "emit": None, "deps": deps,
                             "dma": None, "inc": False, "ord": None,
                             "tag": "join", "dma_full": dma_marks})

    def finalize(self):
        sem_positions = {}
        for o in self.ops:
            if o["dma"] is not None:
                sem_positions.setdefault(o["dma"][0], []).append(o["id"])
        for o in self.ops:
            for d in o["deps"]:
                dep = self.ops[d]
                if dep["emit"] is None:
                    continue
                if dep["dma"] is None and (dep["eng"] != o["eng"]
                                           or dep.get("small")):
                    dep["inc"] = True
        cnt = {e: 0 for e in ("t", "v", "s", "y", "g")}
        for o in self.ops:
            if o["inc"]:
                cnt[o["eng"]] += 1
                o["ord"] = cnt[o["eng"]]
        for o in self.ops:
            waits = {}
            def addw(sem, val):
                if val > waits.get(sem, -1):
                    waits[sem] = val
            for d in o["deps"]:
                dep = self.ops[d]
                if dep["emit"] is None:
                    continue
                if dep["dma"] is not None:
                    name, inc, mode, idx = dep["dma"]
                    if mode == "full":
                        pos = sem_positions[name]
                        tot = bisect.bisect_left(pos, o["id"])
                        addw(name, inc * tot)
                    else:
                        addw(name, inc * idx)
                elif dep["eng"] != o["eng"] or dep.get("small"):
                    addw("E" + dep["eng"], dep["ord"])
            if "dma_full" in o:
                for name, tot in o["dma_full"].items():
                    waits_inc = 1 if name == "cc" else 16
                    addw(name, waits_inc * tot)
            o["waits"] = waits
        return cnt


def _prep(inputs):
    d = {}
    x = np.ascontiguousarray(np.asarray(inputs["L0_attn_x"], np.float32).reshape(16384, 2048))
    d["x_slices"] = [np.ascontiguousarray(x[c * TLOC:(c + 1) * TLOC]) for c in range(NCORE)]

    def t2(w):
        return np.ascontiguousarray(np.asarray(w, np.float32).T)

    def tile4(wt, kchunks, schunks):
        a = wt.reshape(kchunks, 128, schunks, 128)
        return np.ascontiguousarray(a.transpose(2, 1, 0, 3))

    d["wk"] = tile4(t2(inputs["L0_attn_Wk"]), NCH, NHEAD)
    d["wv"] = tile4(t2(inputs["L0_attn_Wv"]), NCH, NHEAD)
    d["wr"] = tile4(t2(inputs["L0_attn_Wr"]), NCH, NHEAD)
    d["wo"] = tile4(t2(inputs["L0_attn_Wo"]), NCH, NCH)

    gw = np.asarray(inputs["L0_ffn_gate_w"], np.float32)
    up = np.asarray(inputs["L0_ffn_up_w"], np.float32)
    dw = np.asarray(inputs["L0_ffn_down_w"], np.float32)
    d["gw"] = np.ascontiguousarray(
        gw.transpose(0, 2, 1).reshape(NEXP, NCH, 128, NF, 128).transpose(0, 3, 2, 1, 4))
    d["up"] = np.ascontiguousarray(
        up.transpose(0, 2, 1).reshape(NEXP, NCH, 128, NF, 128).transpose(0, 3, 2, 1, 4))
    d["dw"] = np.ascontiguousarray(
        dw.transpose(0, 2, 1).reshape(NEXP, NF, 128, NCH, 128).transpose(0, 3, 2, 1, 4))
    sg = t2(inputs["L0_ffn_sh_gate_w"])
    su = t2(inputs["L0_ffn_sh_up_w"])
    sd = t2(inputs["L0_ffn_sh_down_w"])
    d["sg"] = np.ascontiguousarray(sg.reshape(NCH, 128, NFS, 128).transpose(2, 1, 0, 3))
    d["su"] = np.ascontiguousarray(su.reshape(NCH, 128, NFS, 128).transpose(2, 1, 0, 3))
    sd4 = np.ascontiguousarray(sd.reshape(NFS, 128, NCH, 128).transpose(2, 1, 0, 3))
    d["sd"] = np.ascontiguousarray(
        sd4.reshape(NCH, 128, 2, NF, 128).transpose(0, 2, 1, 3, 4))
    gwt = t2(inputs["L0_ffn_gate_weight"])
    d["gwt"] = np.ascontiguousarray(gwt.reshape(NCH, 128, 8).transpose(1, 0, 2))

    d["tdec"] = np.ascontiguousarray(np.asarray(inputs["L0_attn_time_decay"], np.float32))
    tf = np.ascontiguousarray(np.asarray(inputs["L0_attn_time_first"], np.float32))
    z16 = np.zeros_like(tf)
    d["tf_eff"] = [tf if c == 0 else z16 for c in range(NCORE)]

    ev, mv = [], []
    for c in range(NCORE):
        e = np.zeros((1, 8), np.float32)
        m = np.zeros((1, 8), np.float32)
        for r in range(c):
            e[0, r] = float(TLOC * (c - 1 - r))
            m[0, r] = 1.0
        ev.append(e); mv.append(m)
    d["evec"] = ev; d["mvec"] = mv
    d["ident"] = np.eye(128, dtype=np.float32)
    oneh = np.zeros((4, 512), np.float32)
    for g in range(4):
        oneh[g, g * 128:(g + 1) * 128] = 1.0
    d["oneh"] = oneh
    return d


DEBUG_STAGE = None
DEBUG_NJT = 16
DEBUG_TRUNC = None


def _build():
    nc = bass.Bass("TRN2", target_bir_lowering=False)
    ap = {}

    def din(name, shape, dt=F32):
        ap[name] = nc.dram_tensor(name, shape, dt, kind="ExternalInput").ap()
    din("x", [TLOC, C])
    din("wk", [NHEAD, 128, NCH, 128]); din("wv", [NHEAD, 128, NCH, 128])
    din("wr", [NHEAD, 128, NCH, 128]); din("wo", [NCH, 128, NCH, 128])
    din("gw", [NEXP, NF, 128, NCH, 128]); din("up", [NEXP, NF, 128, NCH, 128])
    din("dw", [NEXP, NCH, 128, NF, 128])
    din("sg", [NFS, 128, NCH, 128]); din("su", [NFS, 128, NCH, 128])
    din("sd", [NCH, 2, 128, NF, 128])
    din("gwt", [128, NCH, 8])
    din("tdec", [16, 128]); din("tf_eff", [16, 128])
    din("evec", [1, 8]); din("mvec", [1, 8])
    din("ident", [128, 128]); din("oneh", [4, 512])
    out = nc.dram_tensor("out", [TLOC, C], F32, kind="ExternalOutput").ap()

    dbg = DEBUG_STAGE
    skind = "ExternalOutput" if dbg in ("s1", "s2", "s3") else Ellipsis
    if dbg is not None:
        dbg_xn = nc.dram_tensor("dbg_xn", [128, NCH * 2048], BF16, kind="ExternalOutput").ap()
        dbg_pk = nc.dram_tensor("dbg_pk", [128, 2048], F32, kind="ExternalOutput").ap()
    if dbg in ("s3",):
        dbg_c = nc.dram_tensor("dbg_c", [128, 384], F32, kind="ExternalOutput").ap()
    scr_xn = nc.dram_tensor("scr_xn", [2, 128, NCH, HALF], BF16).ap()
    sk = dict(kind="ExternalOutput") if dbg in ("s2", "s3") else {}
    scr_num = nc.dram_tensor("scr_num", [NHEAD, 128, TLOC], F32, **sk).ap()
    scr_den = nc.dram_tensor("scr_den", [NHEAD, 128, TLOC], F32, **sk).ap()
    scr_sr = nc.dram_tensor("scr_sr", [NHEAD, 128, TLOC], F32, **sk).ap()
    scr_pow = nc.dram_tensor("scr_pow", [NHEAD, 128, TLOC], F32).ap()
    contrib = nc.dram_tensor("contrib", [32, 128], F32).ap()
    gathered = nc.dram_tensor("gathered", [256, 128], F32).ap()

    P = Prog()

    fmap = {}
    _off = [0]
    def fal(name, cols, base=None):
        o = _off[0] if base is None else base
        fmap[name] = (o, cols)
        if base is None:
            _off[0] += cols
    fal("ones", 2048); fal("iota", 1024); fal("idf", 128); fal("pk", 2048)
    fal("gwt", 128); fal("oneh", 512); fal("csm", 448)
    fal("w0all", 64); fal("w1all", 64); fal("a2t", 128)
    G_F32 = _off[0]
    fal("xtm0", 2048, G_F32); fal("xtm1", 2048, G_F32 + 2048)
    fal("xntm0", 2048, G_F32 + 4096); fal("xntm1", 2048, G_F32 + 6144)
    fal("xnf0", 128, G_F32 + 8192); fal("xnf1", 128, G_F32 + 8320)
    fal("xnfF", 2048, G_F32 + 10944)
    fal("dump", 2048, G_F32 + 8448); fal("lns", 64, G_F32 + 10496)
    fal("rsm", 128, G_F32 + 10560)
    szA1 = 12992
    fal("w0", 1024, G_F32); fal("w1", 1024, G_F32 + 1024)
    fal("wv0", 1024, G_F32 + 2048); fal("wv1", 1024, G_F32 + 3072)
    fal("db", 2048, G_F32 + 4096)
    fal("num", 2048, G_F32 + 6144); fal("den", 2048, G_F32 + 8192)
    fal("sr", 2048, G_F32 + 10240); fal("pw2", 2048, G_F32 + 12288)
    szA2 = 14336
    fal("numh", 1024, G_F32); fal("denh", 1024, G_F32 + 1024)
    fal("srh", 1024, G_F32 + 2048); fal("powrec", 1024, G_F32 + 3072)
    fal("obuf", 2048, G_F32 + 4096)
    fal("acc", 16 * 1024, G_F32 + 6144)
    fal("wb", 1024, G_F32)
    szB = 6144 + 16384
    fal("xres0", 2048, G_F32)
    fal("xres1", 2048, G_F32 + 2048)
    fal("outtm", 2048, G_F32 + 4096)
    F32_COLS = G_F32 + max(szA1, szA2, szB)

    bmap = {}
    bmap["xnbf"] = (0, NCH * 2048)
    bo = NCH * 2048
    for nm in ("wkb0", "wkb1", "wvb0", "wvb1", "wrb0", "wrb1"):
        bmap[nm] = (bo, 2048); bo += 2048
    BF_A = bo
    bmap["xnh"] = (0, NCH * HALF)
    r0 = NCH * HALF
    bmap["rwkv"] = (r0, 8 * HALF)
    r1 = r0 + 8 * HALF
    bmap["hp"] = (r1, NF * HALF)
    r2 = r1 + NF * HALF
    bmap["gst0"] = (r2, 2048); bmap["gst1"] = (r2 + 2048, 2048)
    bmap["ust0"] = (r2 + 4096, 2048)
    bmap["dst0"] = (r2 + 6144, NF * 128); bmap["dst1"] = (r2 + 6144 + NF * 128, NF * 128)
    bmap["wob0"] = (r2 + 6144 + 2 * NF * 128, 1024)
    bmap["wob1"] = (r2 + 7168 + 2 * NF * 128, 1024)
    bmap["ustW"] = (r2 + 6144 + 2 * NF * 128, 2048)
    BF_B = r2 + 8192 + 2 * NF * 128
    BF_COLS = max(BF_A, BF_B)
    bpp = F32_COLS * 4 + BF_COLS * 2
    assert bpp <= 207 * 1024, f"SBUF over budget: {bpp} B/partition"

    st = contextlib.ExitStack()
    sems_engine = {}
    sems_dma = {}
    with st:
        AF32 = st.enter_context(nc.sbuf_tensor("AF32", [128, F32_COLS], F32))
        ABF = st.enter_context(nc.sbuf_tensor("ABF", [128, BF_COLS], BF16))
        psum = [st.enter_context(nc.psum_tensor(f"pb{i}", [128, 512], F32))
                for i in range(8)]

        def FB(name, a=0, b=None, p0=0, p1=128):
            o, n = fmap[name]
            if b is None:
                b = n
            return AF32[p0:p1, o + a:o + b]

        def BB(name, a=0, b=None):
            o, n = bmap[name]
            if b is None:
                b = n
            return ABF[:, o + a:o + b]

        for e in ("t", "v", "s", "g"):
            sems_engine[e] = st.enter_context(nc.semaphore(f"se_{e}"))
        for name in ("ldc", "xt0", "xt1", "wk0", "wk1", "wv0", "wv1", "wr0", "wr1",
                     "sxn", "snum", "sden", "ssr", "cwr", "cc", "grd",
                     "xnh", "nh0", "nh1", "dh0", "dh1", "sh0", "sh1", "spow", "ph0", "ph1",
                     "wo0", "wo1", "gs0", "gs1", "us0", "us1", "dsd", "ds1",
                     "xr0", "xr1", "outw"):
            inc = 1 if name == "cc" else 16
            sems_dma[name] = (st.enter_context(nc.semaphore(f"sd_{name}")), inc)

        def E(eng, fn, reads=(), writes=(), tag="", extra_deps=(), small=False):
            return P.op(eng, fn, reads=reads, writes=writes, tag=tag,
                        extra_deps=extra_deps, small=small)

        def DMA(queue, dst, src, sem, mode="slot", reads=(), writes=(), tag="",
                extra_deps=()):
            def fn(e, dst=dst, src=src):
                return e.dma_start(out=dst, in_=src)
            return P.op(queue, fn, reads=reads, writes=writes,
                        dma=(sem, sems_dma[sem][1], mode), tag=tag,
                        extra_deps=extra_deps)

        def last_dma(name):
            for o in reversed(P.ops):
                if o["dma"] is not None and o["dma"][0] == name:
                    return o["id"]
            raise KeyError(name)

        LAM = ("csm", 0, 16); DEC = ("csm", 16, 32); FT = ("csm", 32, 48)
        A2C = ("csm", 48, 80); CAN = ("csm", 80, 96); CAD = ("csm", 96, 112)
        EVE = ("csm", 112, 120); MVE = ("csm", 120, 128)
        TDR = ("csm", 128, 256)
        NL1 = ("csm", 256, 272)
        EBB = ("csm", 272, 280); MBB = ("csm", 280, 288)
        GTMP = ("csm", 288, 304)
        GT2 = ("csm", 304, 320)
        FRT = ("csm", 320, 336); TRT = ("csm", 336, 352)
        FR2 = ("csm", 352, 368)
        CANB = ("csm", 368, 384); CADB = ("csm", 384, 400)
        DP24 = ("csm", 400, 416)

        DMA("y", FB("idf"), ap["ident"], "ldc", "full", writes=["idf"])
        DMA("y", FB("gwt"), ap["gwt"].rearrange("p c e -> p (c e)"), "ldc", "full",
            writes=["gwt"])
        DMA("y", FB("oneh", p0=0, p1=4), ap["oneh"], "ldc", "full", writes=["oneh"])
        DMA("y", FB("oneh", p0=32, p1=36), ap["oneh"], "ldc", "full", writes=["oneh"])
        DMA("y", FB(*EVE, p0=0, p1=1), ap["evec"], "ldc", "full", writes=["ev"])
        DMA("y", FB(*MVE, p0=0, p1=1), ap["mvec"], "ldc", "full", writes=["mv"])
        DMA("y", FB(*TDR, p0=0, p1=16), ap["tdec"], "ldc", "full", writes=["tdraw"])
        E("t", lambda e: e.transpose(psum[0][0:128, 0:16], FB(*TDR, p0=0, p1=16),
                                     FB("idf", 0, 16, p0=0, p1=16)),
          reads=["tdraw", "idf"], writes=["P0"], tag="tdT")
        E("v", lambda e: e.tensor_copy(FB(*LAM), psum[0][:, 0:16]),
          reads=["P0"], writes=["lamr"], small=True)
        DMA("y", FB(*TDR, p0=0, p1=16), ap["tf_eff"], "ldc", "full",
            writes=["tdraw"], tag="tfL")
        E("t", lambda e: e.transpose(psum[1][0:128, 0:16], FB(*TDR, p0=0, p1=16),
                                     FB("idf", 0, 16, p0=0, p1=16)),
          reads=["tdraw", "idf"], writes=["P1"], tag="tfT")
        E("v", lambda e: e.tensor_copy(FB(*FT), psum[1][:, 0:16]),
          reads=["P1"], writes=["ftr"], small=True)
        E("s", lambda e: e.activation(FB(*LAM), FB(*LAM), AF.Exp),
          reads=["lamr"], writes=["lam"], small=True)
        E("s", lambda e: e.activation(FB(*DEC), FB(*LAM), AF.Exp, scale=-1.0),
          reads=["lam"], writes=["dec"], small=True)
        E("s", lambda e: e.activation(FB(*FT), FB(*FT), AF.Exp),
          reads=["ftr"], writes=["ft"], small=True)
        E("v", lambda e: e.memset(FB("ones"), 1.0), writes=["ones"])
        E("v", lambda e: e.memset(FB("pk"), 0.0), writes=["logits", "w0T", "w1T"])
        E("v", lambda e: e.memset(FB("iota"), 0.0), writes=["zer"])

        if dbg == "s0":
            DMA("y", dbg_pk[:, 0:384], AF32[:, fmap["csm"][0]:fmap["csm"][0] + 384], "outw",
                "full", reads=["lam", "dec", "ft", "iota"], tag="dbgc0")
            P.join()
            return _emit(nc, P, sems_engine, sems_dma)

        def LOGITS(a=0, b=2048):
            return FB("pk", a, b, p0=64, p1=72)
        NJ = DEBUG_NJT if dbg in ("s1a", "s1z") else NCH
        E("v", lambda e: e.memset(FB("lns", 16, 32), 0.0), writes=["ssq"])
        for j in range(NJ):
            sl = j % 2
            DMA("y", FB(f"xtm{sl}"), ap["x"][j * 128:(j + 1) * 128, :], f"xt{sl}",
                writes=[f"xtm{sl}"], tag=f"xLa{j}")
            E("v", lambda e, sl=sl, j=j: e.reduce_sum(FB("lns", j, j + 1),
                                                      FB(f"xtm{sl}"), axis=AX),
              reads=[f"xtm{sl}"], writes=["msum"], small=(j == NJ - 1), tag=f"rs{j}")
            E("s", lambda e, sl=sl, j=j: e.activation(
                FB("dump"), FB(f"xtm{sl}"), AF.Square, accum_out=FB("lns", 16 + j, 17 + j)),
              reads=[f"xtm{sl}", "ssq"], writes=["dump", "ssq"],
              small=(j == NJ - 1), tag=f"sq{j}")
        E("v", lambda e: e.tensor_scalar(FB("rsm", 0, 16), FB("lns", 0, 16),
                                         1.0 / 2048, None, ALU.mult),
          reads=["msum"], writes=["m"], small=True)
        E("v", lambda e: e.tensor_tensor(FB("rsm", 16, 32), FB("rsm", 0, 16),
                                         FB("rsm", 0, 16), ALU.mult),
          reads=["m"], writes=["m2"], small=True)
        E("v", lambda e: e.tensor_scalar(FB("rsm", 32, 48), FB("rsm", 16, 32),
                                         -1.0, None, ALU.mult),
          reads=["m2"], writes=["nm2"], small=True)
        E("v", lambda e: e.tensor_scalar(FB("rsm", 48, 64), FB("lns", 16, 32),
                                         1.0 / 2048, None, ALU.mult),
          reads=["ssq"], writes=["ex2"], small=True)
        E("v", lambda e: e.tensor_tensor(FB("rsm", 64, 80), FB("rsm", 48, 64),
                                         FB("rsm", 32, 48), ALU.add),
          reads=["ex2", "nm2"], writes=["varb"], small=True)
        E("v", lambda e: e.tensor_scalar(FB("rsm", 80, 96), FB("rsm", 64, 80),
                                         1e-5, None, ALU.add),
          reads=["varb"], writes=["vare"], small=True)
        E("s", lambda e: e.activation(FB("rsm", 96, 112), FB("rsm", 80, 96), AF.Ln),
          reads=["vare"], writes=["lnv"], small=True)
        E("s", lambda e: e.activation(FB("lns", 32, 48), FB("rsm", 96, 112), AF.Exp,
                                      scale=-0.5),
          reads=["lnv"], writes=["rstd"], small=True)
        E("v", lambda e: e.tensor_scalar(FB("rsm", 112, 128), FB("rsm", 0, 16),
                                         -1.0, None, ALU.mult),
          reads=["m"], writes=["negm"], small=True)
        E("v", lambda e: e.tensor_tensor(FB("lns", 48, 64), FB("rsm", 112, 128),
                                         FB("lns", 32, 48), ALU.mult),
          reads=["negm", "rstd"], writes=["nmr"], small=True)
        for j in range(NJ):
            sl = j % 2
            DMA("y", FB(f"xtm{sl}"), ap["x"][j * 128:(j + 1) * 128, :], f"xt{sl}",
                writes=[f"xtm{sl}"], tag=f"xLb{j}")
            E("v", lambda e, sl=sl, j=j: e.tensor_scalar(
                FB(f"xntm{sl}"), FB(f"xtm{sl}"), FB("lns", 32 + j, 33 + j),
                FB("lns", 48 + j, 49 + j), ALU.mult, ALU.add),
              reads=[f"xtm{sl}", "rstd", "nmr"], writes=[f"xntm{sl}"])
            for i in range(NCH):
                bk = i % 4
                E("t", lambda e, i=i, bk=bk, sl=sl: e.transpose(
                    psum[bk][:, 0:128], FB(f"xntm{sl}", i * 128, (i + 1) * 128),
                    FB("idf")),
                  reads=[f"xntm{sl}", "idf"], writes=[f"P{bk}"], tag=f"xT{j}_{i}")
                E("s", lambda e, bk=bk, i=i: e.copy(
                    FB("xnfF", i * 128, (i + 1) * 128), psum[bk][:, 0:128]),
                  reads=[f"P{bk}"], writes=[f"xnfF{i}"])
                E("v", lambda e, i=i, j=j: e.tensor_copy(
                    BB("xnbf", i * 2048 + j * 128, i * 2048 + (j + 1) * 128),
                    FB("xnfF", i * 128, (i + 1) * 128)),
                  reads=[f"xnfF{i}"], writes=[f"xnbf{i}"])
            for i in range(NCH):
                E("t", lambda e, i=i: e.matmul(
                    psum[4][0:8, 0:128], FB("gwt", i * 8, (i + 1) * 8),
                    FB("xnfF", i * 128, (i + 1) * 128),
                    start=(i == 0), stop=(i == NCH - 1)),
                  reads=[f"xnfF{i}", "gwt"], writes=["P4"], tag=f"gate{j}_{i}")
            E("v", lambda e, j=j: e.tensor_copy(LOGITS(j * 128, (j + 1) * 128),
                                                psum[4][0:8, 0:128]),
              reads=["P4"], writes=["logits"])
        if dbg == "s1z":
            DMA("g", dbg_xn[:, 0:2048], FB("xtm0"), "outw", "full",
                reads=["xtm0"], tag="dz1")
            DMA("g", dbg_xn[:, 2048:4096], FB("xntm0"), "outw", "full",
                reads=["xntm0"], tag="dz2")
            DMA("g", dbg_xn[:, 4096:4160], FB("lns"), "outw", "full",
                reads=["m", "ssq", "varb", "rstd", "nmr"], tag="dz3")
            DMA("y", dbg_xn[:, 6144:8192], BB("xnbf", 0, 2048), "outw", "full",
                reads=["xnbf0"], tag="dz4")
            P.join()
            return _emit(nc, P, sems_engine, sems_dma)

        if dbg == "s1a":
            DMA("y", dbg_xn, BB("xnbf"), "outw", "full",
                reads=[f"xnbf{i}" for i in range(NCH)], tag="dbgxn")
            DMA("y", dbg_pk, AF32[:, fmap["pk"][0]:fmap["pk"][0] + 2048], "outw",
                "full", reads=["logits"], tag="dbgpk")
            P.join()
            return _emit(nc, P, sems_engine, sems_dma)

        for half in range(2):
            for i in range(NCH):
                DMA("y", scr_xn[half, :, i, :],
                    BB("xnbf", i * 2048 + half * HALF, i * 2048 + half * HALF + HALF),
                    "sxn", "full", reads=[f"xnbf{i}"], tag=f"sxnw{half}_{i}")

        def _dbg_dump_s1x():
            DMA("y", dbg_xn, BB("xnbf"), "outw", "full",
                reads=[f"xnbf{i}" for i in range(NCH)], tag="dbgxn")
            DMA("y", dbg_pk, AF32[:, fmap["pk"][0]:fmap["pk"][0] + 2048], "outw",
                "full", reads=["logits"], tag="dbgpk")
            P.join()

        if dbg == "s1b":
            _dbg_dump_s1x()
            return _emit(nc, P, sems_engine, sems_dma)

        def R(a, b):
            return FB("rsm", a, b)
        for j in range(NCH):
            E("t", lambda e, j=j: e.transpose(psum[5][0:128, 0:8],
                                              LOGITS(j * 128, (j + 1) * 128),
                                              FB("idf", 64, 72, p0=64, p1=72)),
              reads=["logits", "idf"], writes=["P5"], tag=f"lgT{j}")
            E("v", lambda e: e.tensor_copy(R(0, 8), psum[5][:, 0:8]),
              reads=["P5"], writes=["lgtm"], small=True)
            E("v", lambda e: e.reduce_max(R(8, 9), R(0, 8), axis=AX),
              reads=["lgtm"], writes=["mx"], small=True)
            E("v", lambda e: e.tensor_scalar(R(9, 10), R(8, 9), -1.0, None, ALU.mult),
              reads=["mx"], writes=["nmx"], small=True)
            E("v", lambda e: e.tensor_scalar(R(10, 18), R(0, 8), R(9, 10), None, ALU.add),
              reads=["lgtm", "nmx"], writes=["sh"], small=True)
            E("s", lambda e: e.activation(R(18, 26), R(10, 18), AF.Exp),
              reads=["sh"], writes=["exq"], small=True)
            E("v", lambda e: e.reduce_sum(R(26, 27), R(18, 26), axis=AX),
              reads=["exq"], writes=["ssum"], small=True)
            E("v", lambda e: e.reciprocal(R(27, 28), R(26, 27)),
              reads=["ssum"], writes=["rsum"], small=True)
            E("v", lambda e: e.tensor_scalar(R(28, 36), R(18, 26), R(27, 28), None,
                                             ALU.mult),
              reads=["exq", "rsum"], writes=["sc8"], small=True)
            e8 = R(28, 36).rearrange("p (g t) -> p g t", t=2)
            E("v", lambda e, e8=e8: e.tensor_tensor(R(36, 40), e8[:, :, 0],
                                                    e8[:, :, 1], ALU.max),
              reads=["sc8"], writes=["g4"], small=True)
            E("v", lambda e: e.reduce_max(R(40, 41), R(36, 40), axis=AX),
              reads=["g4"], writes=["gm1"], small=True)
            E("v", lambda e: e.tensor_scalar(R(41, 45), R(36, 40), R(40, 41), None,
                                             ALU.is_ge),
              reads=["g4", "gm1"], writes=["i1"], small=True)
            E("v", lambda e: e.scalar_tensor_tensor(R(45, 49), R(41, 45), -1e30,
                                                    R(36, 40), ALU.mult, ALU.add),
              reads=["i1", "g4"], writes=["gneg"], small=True)
            E("v", lambda e: e.reduce_max(R(49, 50), R(45, 49), axis=AX),
              reads=["gneg"], writes=["gm2"], small=True)
            E("v", lambda e: e.tensor_scalar(R(50, 54), R(36, 40), R(49, 50), None,
                                             ALU.is_ge),
              reads=["g4", "gm2"], writes=["gmask"], small=True)
            E("v", lambda e, e8=e8: e.tensor_tensor(R(54, 58), e8[:, :, 0],
                                                    R(50, 54), ALU.mult),
              reads=["sc8", "gmask"], writes=["m0"], small=True)
            E("v", lambda e, e8=e8: e.tensor_tensor(R(58, 62), e8[:, :, 1],
                                                    R(50, 54), ALU.mult),
              reads=["sc8", "gmask"], writes=["m1"], small=True)
            E("v", lambda e: e.reduce_max(R(62, 63), R(54, 58), axis=AX),
              reads=["m0"], writes=["t1a"], small=True)
            E("v", lambda e: e.reduce_max(R(63, 64), R(58, 62), axis=AX),
              reads=["m1"], writes=["t1b"], small=True)
            E("v", lambda e: e.tensor_tensor(R(64, 65), R(62, 63), R(63, 64), ALU.max),
              reads=["t1a", "t1b"], writes=["t1"], small=True)
            E("v", lambda e: e.tensor_scalar(R(65, 69), R(54, 58), R(64, 65), None,
                                             ALU.is_ge),
              reads=["m0", "t1"], writes=["k0a"], small=True)
            E("v", lambda e: e.scalar_tensor_tensor(R(69, 73), R(65, 69), -1e30,
                                                    R(54, 58), ALU.mult, ALU.add),
              reads=["k0a", "m0"], writes=["k0n"], small=True)
            E("v", lambda e: e.tensor_scalar(R(73, 77), R(58, 62), R(64, 65), None,
                                             ALU.is_ge),
              reads=["m1", "t1"], writes=["k1a"], small=True)
            E("v", lambda e: e.scalar_tensor_tensor(R(77, 81), R(73, 77), -1e30,
                                                    R(58, 62), ALU.mult, ALU.add),
              reads=["k1a", "m1"], writes=["k1n"], small=True)
            E("v", lambda e: e.reduce_max(R(81, 82), R(69, 73), axis=AX),
              reads=["k0n"], writes=["t2a"], small=True)
            E("v", lambda e: e.reduce_max(R(82, 83), R(77, 81), axis=AX),
              reads=["k1n"], writes=["t2b"], small=True)
            E("v", lambda e: e.tensor_tensor(R(83, 84), R(81, 82), R(82, 83), ALU.max),
              reads=["t2a", "t2b"], writes=["t2"], small=True)
            E("v", lambda e: e.tensor_scalar(R(84, 88), R(54, 58), R(83, 84), None,
                                             ALU.is_ge),
              reads=["m0", "t2"], writes=["k0"], small=True)
            E("v", lambda e: e.tensor_scalar(R(88, 92), R(58, 62), R(83, 84), None,
                                             ALU.is_ge),
              reads=["m1", "t2"], writes=["k1"], small=True)
            E("v", lambda e, j=j: e.tensor_tensor(FB("w0all", j * 4, (j + 1) * 4),
                                                  R(54, 58), R(84, 88), ALU.mult),
              reads=["m0", "k0"], writes=["w0all"], small=True)
            E("v", lambda e, j=j: e.tensor_tensor(FB("w1all", j * 4, (j + 1) * 4),
                                                  R(58, 62), R(88, 92), ALU.mult),
              reads=["m1", "k1"], writes=["w1all"], small=True)
        for j in range(NCH):
            E("t", lambda e, j=j: e.transpose(psum[6][0:4, 0:128],
                                              FB("w0all", j * 4, (j + 1) * 4),
                                              FB("idf")),
              reads=["w0all", "idf"], writes=["P6"], tag=f"w0T{j}")
            E("v", lambda e, j=j: e.tensor_copy(
                FB("pk", j * 128, (j + 1) * 128, p0=0, p1=4), psum[6][0:4, 0:128]),
              reads=["P6"], writes=["w0T"])
            E("t", lambda e, j=j: e.transpose(psum[7][0:4, 0:128],
                                              FB("w1all", j * 4, (j + 1) * 4),
                                              FB("idf")),
              reads=["w1all", "idf"], writes=["P7"], tag=f"w1T{j}")
            E("v", lambda e, j=j: e.tensor_copy(
                FB("pk", j * 128, (j + 1) * 128, p0=32, p1=36), psum[7][0:4, 0:128]),
              reads=["P7"], writes=["w1T"])

        P.join()

        def _dbg_dump_s1():
            DMA("y", dbg_xn, BB("xnbf"), "outw", "full",
                reads=[f"xnbf{i}" for i in range(NCH)], tag="dbgxn")
            DMA("y", dbg_pk, AF32[:, fmap["pk"][0]:fmap["pk"][0] + 2048], "outw",
                "full", reads=["logits", "w0T", "w1T"], tag="dbgpk")
            P.join()

        if dbg == "s1":
            _dbg_dump_s1()
            return _emit(nc, P, sems_engine, sems_dma)

        def wload(h, sl):
            for nm in ("wk", "wv", "wr"):
                DMA("g", BB(f"{nm}b{sl}").rearrange("p (c f) -> p c f", c=NCH),
                    ap[nm][h], f"{nm}{sl}", writes=[f"{nm}b{sl}"], tag=f"{nm}L{h}")
        wload(0, 0)
        wload(1, 1)
        for h in range(NHEAD):
            sl = h % 2
            E("v", lambda e, h=h: e.tensor_scalar(FB("db"), FB("ones"),
                                                  FB(*DEC)[:, h:h + 1], None, ALU.mult),
              reads=["ones", "dec"], writes=["db"], tag=f"db{h}")
            for ts in range(4):
                wsl = FB(f"w{ts // 2}", (ts % 2) * 512, (ts % 2) * 512 + 512)
                wvsl = FB(f"wv{ts // 2}", (ts % 2) * 512, (ts % 2) * 512 + 512)
                bk, bv, br = ts % 2, 2 + ts % 2, 4 + ts % 2
                for i in range(NCH):
                    E("t", lambda e, i=i, ts=ts, bk=bk, sl=sl: e.matmul(
                        psum[bk][:, :], BB(f"wkb{sl}", i * 128, (i + 1) * 128),
                        BB("xnbf", i * 2048 + ts * 512, i * 2048 + (ts + 1) * 512),
                        start=(i == 0), stop=(i == NCH - 1)),
                      reads=[f"wkb{sl}", f"xnbf{i}"], writes=[f"P{bk}"],
                      tag=f"k{h}_{ts}_{i}")
                E("s", lambda e, wsl=wsl, bk=bk: e.activation(wsl, psum[bk][:, :], AF.Exp),
                  reads=[f"P{bk}"], writes=[f"wq{ts}"], tag=f"wexp{h}_{ts}")
                E("v", lambda e, wsl=wsl: e.tensor_scalar(wsl, wsl, E10, EM10,
                                                          ALU.min, ALU.max),
                  reads=[f"wq{ts}"], writes=[f"wq{ts}"], tag=f"wclip{h}_{ts}")
                if ts == 0:
                    E("v", lambda e, h=h: e.tensor_tensor(
                        FB(*NL1)[:, 2:3], FB("w0", 0, 1), FB(*FT)[:, h:h + 1],
                        ALU.mult),
                      reads=["wq0", "ft"], writes=["fm"], small=True,
                      tag=f"fm{h}")
                    E("v", lambda e: e.tensor_copy(FB("w0", 0, 1), FB(*NL1)[:, 2:3]),
                      reads=["fm"], writes=["wq0"], small=True, tag=f"fmc{h}")
                for i in range(NCH):
                    E("t", lambda e, i=i, ts=ts, bv=bv, sl=sl: e.matmul(
                        psum[bv][:, :], BB(f"wvb{sl}", i * 128, (i + 1) * 128),
                        BB("xnbf", i * 2048 + ts * 512, i * 2048 + (ts + 1) * 512),
                        start=(i == 0), stop=(i == NCH - 1)),
                      reads=[f"wvb{sl}", f"xnbf{i}"], writes=[f"P{bv}"],
                      tag=f"v{h}_{ts}_{i}")
                E("v", lambda e, wsl=wsl, wvsl=wvsl, bv=bv: e.tensor_tensor(
                    wvsl, psum[bv][:, :], wsl, ALU.mult),
                  reads=[f"P{bv}", f"wq{ts}"], writes=[f"wvq{ts}"], tag=f"wv{h}_{ts}")
                for i in range(NCH):
                    E("t", lambda e, i=i, ts=ts, br=br, sl=sl: e.matmul(
                        psum[br][:, :], BB(f"wrb{sl}", i * 128, (i + 1) * 128),
                        BB("xnbf", i * 2048 + ts * 512, i * 2048 + (ts + 1) * 512),
                        start=(i == 0), stop=(i == NCH - 1)),
                      reads=[f"wrb{sl}", f"xnbf{i}"], writes=[f"P{br}"],
                      tag=f"r{h}_{ts}_{i}")
                E("s", lambda e, ts=ts, br=br: e.activation(
                    FB("sr", ts * 512, (ts + 1) * 512), psum[br][:, :], AF.Tanh,
                    scale=0.5),
                  reads=[f"P{br}"], writes=["srq"], tag=f"tanh{h}_{ts}")
            if h + 2 < NHEAD:
                wload(h + 2, (h + 2) % 2)
            E("v", lambda e: e.tensor_tensor_scan(
                FB("num", 0, 1024), FB("db", 0, 1024), FB("w0" if False else "wv0"),
                0.0, ALU.mult, ALU.add),
              reads=["db", "wvq0", "wvq1"], writes=["numq0"], small=True,
              tag=f"scN0_{h}")
            E("v", lambda e: e.tensor_tensor_scan(
                FB("num", 1024, 2048), FB("db", 1024, 2048), FB("wv1"),
                FB("num", 1023, 1024), ALU.mult, ALU.add),
              reads=["db", "wvq2", "wvq3", "numq0"], writes=["numq1"], small=True,
              tag=f"scN1_{h}")
            E("v", lambda e: e.tensor_tensor_scan(
                FB("den", 0, 1024), FB("db", 0, 1024), FB("w0"),
                0.0, ALU.mult, ALU.add),
              reads=["db", "wq0", "wq1"], writes=["denq0"], small=True,
              tag=f"scD0_{h}")
            E("v", lambda e: e.tensor_tensor_scan(
                FB("den", 1024, 2048), FB("db", 1024, 2048), FB("w1"),
                FB("den", 1023, 1024), ALU.mult, ALU.add),
              reads=["db", "wq2", "wq3", "denq0"], writes=["denq1"], small=True,
              tag=f"scD1_{h}")
            E("v", lambda e, h=h: e.tensor_copy(FB(*A2C)[:, h:h + 1],
                                                FB("num", 2047, 2048)),
              reads=["numq1"], writes=["A2"], small=True, tag=f"An{h}")
            E("v", lambda e, h=h: e.tensor_copy(FB(*A2C)[:, 16 + h:17 + h],
                                                FB("den", 2047, 2048)),
              reads=["denq1"], writes=["A2"], small=True, tag=f"Ad{h}")
            E("v", lambda e: e.tensor_scalar(FB("den"), FB("den"), 1e-6, None,
                                             ALU.add),
              reads=["denq0", "denq1"], writes=["denq0", "denq1"], tag=f"depsA{h}")
            E("v", lambda e: e.tensor_tensor_scan(
                FB("pw2", 0, 1024), FB("db", 0, 1024), FB("iota"),
                1.0, ALU.mult, ALU.add),
              reads=["db", "zer"], writes=["pwq0"], small=True, tag=f"scP0_{h}")
            E("v", lambda e: e.tensor_tensor_scan(
                FB("pw2", 1024, 2048), FB("db", 1024, 2048), FB("iota"),
                FB("pw2", 1023, 1024), ALU.mult, ALU.add),
              reads=["db", "zer", "pwq0"], writes=["pwq1"], small=True,
              tag=f"scP1_{h}")
            DMA("y", scr_pow[h], FB("pw2"), "spow", "full",
                reads=["pwq0", "pwq1"], tag=f"spow{h}")
            DMA("y", scr_num[h], FB("num"), "snum", "full",
                reads=["numq0", "numq1"], tag=f"snum{h}")
            DMA("y", scr_den[h], FB("den"), "sden", "full",
                reads=["denq0", "denq1"], tag=f"sden{h}")
            DMA("y", scr_sr[h], FB("sr"), "ssr", "full", reads=["srq"], tag=f"ssr{h}")

        P.join()

        E("t", lambda e: e.transpose(psum[6][0:32, 0:128], FB(*A2C), FB("idf")),
          reads=["A2", "idf"], writes=["P6"], tag="A2T")
        E("v", lambda e: e.tensor_copy(FB("a2t", 0, 128, p0=0, p1=32),
                                       psum[6][0:32, 0:128]),
          reads=["P6"], writes=["A2T"], small=True, tag="A2Tc")
        DMA("y", contrib, FB("a2t", 0, 128, p0=0, p1=32), "cwr", "full",
            reads=["A2T"], writes=["contribd"], tag="cwr")
        def ccfn(e):
            return e.collective_compute(
                "AllGather", ALU.bypass, replica_groups=[list(range(NCORE))],
                ins=[contrib.opt()], outs=[gathered.opt()])
        P.op("g", ccfn, reads=["contribd"], writes=["gatheredd"],
             dma=("cc", 1, "slot"), tag="allgather")
        E("t", lambda e: e.matmul(psum[7][:, 0:8], FB("ones", 0, 128, p0=0, p1=1),
                                  FB(*EVE, p0=0, p1=1), start=True, stop=True),
          reads=["ones", "ev"], writes=["P7"], tag="ebB")
        E("v", lambda e: e.tensor_copy(FB(*EBB), psum[7][:, 0:8]),
          reads=["P7"], writes=["ebb"], small=True, tag="ebc")
        E("t", lambda e: e.matmul(psum[7][:, 0:8], FB("ones", 0, 128, p0=0, p1=1),
                                  FB(*MVE, p0=0, p1=1), start=True, stop=True),
          reads=["ones", "mv"], writes=["P7"], tag="mbB")
        E("v", lambda e: e.tensor_copy(FB(*MBB), psum[7][:, 0:8]),
          reads=["P7"], writes=["mbb"], small=True, tag="mbc")
        E("v", lambda e: e.memset(FB(*CAN), 0.0), writes=["can0"], small=True)
        E("v", lambda e: e.memset(FB(*CAD), 0.0), writes=["cad0"], small=True)
        for r in range(NCORE - 1):
            pa = r % 2
            CNa, CNb = (CAN, CANB) if pa == 0 else (CANB, CAN)
            CDa, CDb = (CAD, CADB) if pa == 0 else (CADB, CAD)
            ka, kb = (f"can{r}", f"can{r + 1}"), None
            DMA("y", FB(*TDR, p0=0, p1=16), gathered[r * 32:r * 32 + 16, :],
                "grd", "full", reads=["gatheredd"], writes=["tdraw"], tag=f"gn{r}")
            E("t", lambda e: e.transpose(psum[6][0:128, 0:16], FB(*TDR, p0=0, p1=16),
                                         FB("idf", 0, 16, p0=0, p1=16)),
              reads=["tdraw", "idf"], writes=["P6"], tag=f"gnT{r}")
            E("v", lambda e: e.tensor_copy(FB(*GTMP), psum[6][:, 0:16]),
              reads=["P6"], writes=["anr"], small=True, tag=f"gnc{r}")
            DMA("y", FB(*TDR, p0=0, p1=16), gathered[r * 32 + 16:r * 32 + 32, :],
                "grd", "full", reads=["gatheredd"], writes=["tdraw"], tag=f"gd{r}")
            E("t", lambda e: e.transpose(psum[6][0:128, 0:16], FB(*TDR, p0=0, p1=16),
                                         FB("idf", 0, 16, p0=0, p1=16)),
              reads=["tdraw", "idf"], writes=["P6"], tag=f"gdT{r}")
            E("v", lambda e: e.tensor_copy(FB(*GT2), psum[6][:, 0:16]),
              reads=["P6"], writes=["adr"], small=True, tag=f"gdc{r}")
            E("v", lambda e, r=r: e.tensor_scalar(FB(*FRT), FB(*LAM),
                                                  FB(*EBB)[:, r:r + 1], None, ALU.mult),
              reads=["lam", "ebb"], writes=["fr"], small=True, tag=f"f1_{r}")
            E("s", lambda e: e.activation(FB(*FR2), FB(*FRT), AF.Exp, scale=-1.0),
              reads=["fr"], writes=["fr2"], small=True, tag=f"f2_{r}")
            E("v", lambda e, r=r: e.tensor_scalar(FB(*FRT), FB(*FR2),
                                                  FB(*MBB)[:, r:r + 1], None, ALU.mult),
              reads=["fr2", "mbb"], writes=["fr"], small=True, tag=f"f3_{r}")
            E("v", lambda e: e.tensor_tensor(FB(*TRT), FB(*FRT), FB(*GTMP), ALU.mult),
              reads=["fr", "anr"], writes=["tnr"], small=True, tag=f"f4_{r}")
            E("v", lambda e, CNa=CNa, CNb=CNb: e.tensor_tensor(
                FB(*CNb), FB(*CNa), FB(*TRT), ALU.add),
              reads=["tnr", f"can{r}"], writes=[f"can{r + 1}"], small=True,
              tag=f"f5_{r}")
            E("v", lambda e: e.tensor_tensor(FB(*TRT), FB(*FRT), FB(*GT2), ALU.mult),
              reads=["fr", "adr"], writes=["tdr"], small=True, tag=f"f6_{r}")
            E("v", lambda e, CDa=CDa, CDb=CDb: e.tensor_tensor(
                FB(*CDb), FB(*CDa), FB(*TRT), ALU.add),
              reads=["tdr", f"cad{r}"], writes=[f"cad{r + 1}"], small=True,
              tag=f"f7_{r}")
        if (NCORE - 1) % 2 == 1:
            E("v", lambda e: e.tensor_copy(FB(*CAN), FB(*CANB)),
              reads=[f"can{NCORE - 1}"], writes=["can"], small=True, tag="canf")
            E("v", lambda e: e.tensor_copy(FB(*CAD), FB(*CADB)),
              reads=[f"cad{NCORE - 1}"], writes=["cad"], small=True, tag="cadf")
        else:
            E("v", lambda e: e.tensor_copy(FB(*CANB), FB(*CAN)),
              reads=[f"can{NCORE - 1}"], writes=["can"], small=True, tag="canf")
            E("v", lambda e: e.tensor_copy(FB(*CADB), FB(*CAD)),
              reads=[f"cad{NCORE - 1}"], writes=["cad"], small=True, tag="cadf")

        if dbg == "s2":
            _dbg_dump_s1()
            return _emit(nc, P, sems_engine, sems_dma)
        if dbg == "s3":
            _dbg_dump_s1()
            DMA("y", dbg_c, AF32[:, fmap["csm"][0]:fmap["csm"][0] + 384], "outw",
                "full", reads=["can", "cad", "lam", "dec", "A2"], tag="dbgc")
            P.join()
            return _emit(nc, P, sems_engine, sems_dma)

        P.join()

        WBK = ("wb", "numh")
        PWB = lambda: FB("obuf", 1024, 2048)
        XR0 = ("xres0", "numh", "denh")
        XR1 = ("xres1", "srh", "pow")
        OTM = ("outtm", "gsil", "pwb")

        def emit_corr(half, h, piece=0):
            hc0 = half * HALF
            slot = h % 8
            sl = h % 2
            if piece in (0, 1):
                _corr_p1(half, h, hc0, sl)
            if piece in (0, 2):
                _corr_p2(half, h)
            if piece in (0, 3):
                _corr_p3(half, h, slot)

        def _corr_p1(half, h, hc0, sl):
            DMA("y", FB("numh"), scr_num[h][:, hc0:hc0 + HALF], f"nh{sl}",
                writes=["numh"], extra_deps=(last_dma("snum"),), tag=f"nhL{half}_{h}")
            DMA("y", FB("denh"), scr_den[h][:, hc0:hc0 + HALF], f"dh{sl}",
                writes=["denh"], extra_deps=(last_dma("sden"),), tag=f"dhL{half}_{h}")
            DMA("y", FB("srh"), scr_sr[h][:, hc0:hc0 + HALF], f"sh{sl}",
                writes=["srh"], extra_deps=(last_dma("ssr"),), tag=f"shL{half}_{h}")
            DMA("y", FB("powrec"), scr_pow[h][:, hc0:hc0 + HALF], f"ph{sl}",
                writes=["pow"], extra_deps=(last_dma("spow"),),
                tag=f"phL{half}_{h}")
            E("v", lambda e, h=h: e.scalar_tensor_tensor(
                PWB(), FB("powrec"), FB(*CAN)[:, h:h + 1], FB("numh"),
                ALU.mult, ALU.add),
              reads=["pow", "can", "numh"], writes=["pwb"], tag=f"cn{half}_{h}")
            E("v", lambda e, h=h: e.scalar_tensor_tensor(
                FB("numh"), FB("powrec"), FB(*CAD)[:, h:h + 1], FB("denh"),
                ALU.mult, ALU.add),
              reads=["pow", "cad", "denh"], writes=["numh"], tag=f"cd{half}_{h}")

        def _corr_p2(half, h):
            E("v", lambda e: e.reciprocal(FB("denh", 0, 512), FB("numh", 0, 512)),
              reads=["numh"], writes=["denh"], tag=f"recA{half}_{h}")
            E("v", lambda e: e.reciprocal(FB("denh", 512, 1024), FB("numh", 512, 1024)),
              reads=["numh"], writes=["denh"], tag=f"recB{half}_{h}")
            E("g", lambda e: e.tensor_tensor(FB("numh"), PWB(), FB("denh"),
                                             ALU.mult),
              reads=["pwb", "denh"], writes=["numh"], tag=f"wkv{half}_{h}")

        def _corr_p3(half, h, slot):
            E("g", lambda e: e.tensor_scalar(FB("powrec"), FB("srh"), 1.0, 0.5,
                                             ALU.add, ALU.mult),
              reads=["srh", "pow"], writes=["pow"], tag=f"sig{half}_{h}")
            E("v", lambda e, slot=slot: e.tensor_tensor(
                BB("rwkv", slot * HALF, (slot + 1) * HALF), FB("numh"), FB("powrec"),
                ALU.mult),
              reads=["numh", "pow"], writes=[f"rk{slot}"], tag=f"rk{half}_{h}")

        def emit_gu(half, tagp, gsrc, usrc, fj, hpj, wbmul):
            sl = fj % 2
            if wbmul and fj % 2 == 1:
                ubuf, usem, ukeys = "ustW", "us1", ["ustW", "wob0", "wob1"]
            else:
                ubuf, usem, ukeys = "ust0", "us0", ["ust0"]
            DMA("g", BB(f"gst{sl}").rearrange("p (c f) -> p c f", c=NCH),
                gsrc, f"gs{sl}", writes=[f"gst{sl}"], tag=f"gL{tagp}")
            DMA("g", BB(ubuf).rearrange("p (c f) -> p c f", c=NCH),
                usrc, usem, writes=ukeys, tag=f"uL{tagp}")
            bg = 2 * (fj % 2)
            bu = 6 - 2 * (fj % 2)
            for ts in range(2):
                for i in range(NCH):
                    E("t", lambda e, i=i, ts=ts, sl=sl, bg=bg: e.matmul(
                        psum[bg + ts][:, :], BB(f"gst{sl}", i * 128, (i + 1) * 128),
                        BB("xnh", i * HALF + ts * 512, i * HALF + (ts + 1) * 512),
                        start=(i == 0), stop=(i == NCH - 1)),
                      reads=[f"gst{sl}", "xnh"], writes=[f"P{bg + ts}"],
                      tag=f"g{tagp}_{ts}_{i}")
                E("s", lambda e, ts=ts, bg=bg: e.activation(
                    FB("obuf", ts * 512, (ts + 1) * 512), psum[bg + ts][:, :], AF.Silu),
                  reads=[f"P{bg + ts}"], writes=["gsil"], tag=f"sil{tagp}_{ts}")
                for i in range(NCH):
                    E("t", lambda e, i=i, ts=ts, ubuf=ubuf, bu=bu: e.matmul(
                        psum[bu + ts][:, :], BB(ubuf, i * 128, (i + 1) * 128),
                        BB("xnh", i * HALF + ts * 512, i * HALF + (ts + 1) * 512),
                        start=(i == 0), stop=(i == NCH - 1)),
                      reads=list(ukeys) + ["xnh"], writes=[f"P{bu + ts}"],
                      tag=f"u{tagp}_{ts}_{i}")
                if wbmul:
                    E("v", lambda e, ts=ts, bu=bu: e.tensor_tensor(
                        FB("obuf", ts * 512, (ts + 1) * 512), psum[bu + ts][:, :],
                        FB("obuf", ts * 512, (ts + 1) * 512), ALU.mult),
                      reads=[f"P{bu + ts}", "gsil"], writes=["gsil"],
                      tag=f"gu{tagp}_{ts}")
                else:
                    E("v", lambda e, ts=ts, hpj=hpj, bu=bu: e.tensor_tensor(
                        BB("hp", hpj * HALF + ts * 512, hpj * HALF + (ts + 1) * 512),
                        psum[bu + ts][:, :], FB("obuf", ts * 512, (ts + 1) * 512),
                        ALU.mult),
                      reads=[f"P{bu + ts}", "gsil"], writes=[f"hp{hpj}"],
                      tag=f"gu{tagp}_{ts}")
            if wbmul:
                E("v", lambda e, hpj=hpj: e.tensor_tensor(
                    BB("hp", hpj * HALF, (hpj + 1) * HALF), FB("obuf", 0, 1024),
                    FB("wb", 0, 1024), ALU.mult),
                  reads=["gsil"] + list(WBK), writes=[f"hp{hpj}"], tag=f"hp{tagp}")

        def emit_down(half, tagp, dsrc, co, acc_first):
            sl = co % 2
            DMA("g", BB(f"dst{sl}").rearrange("p (c f) -> p c f", c=NF),
                dsrc, "dsd" if sl == 0 else "ds1", writes=[f"dst{sl}"],
                tag=f"dL{tagp}")
            for ts in range(2):
                bk = 6 + ts
                for fj in range(NF):
                    E("t", lambda e, fj=fj, ts=ts, bk=bk, sl=sl: e.matmul(
                        psum[bk][:, :], BB(f"dst{sl}", fj * 128, (fj + 1) * 128),
                        BB("hp", fj * HALF + ts * 512, fj * HALF + (ts + 1) * 512),
                        start=(fj == 0), stop=(fj == NF - 1)),
                      reads=[f"dst{sl}", f"hp{fj}"], writes=[f"P{bk}"],
                      tag=f"d{tagp}_{ts}_{fj}")
                if acc_first.pop((co, ts), False):
                    E("v", lambda e, co=co, ts=ts, bk=bk: e.tensor_copy(
                        FB("acc", co * 1024 + ts * 512, co * 1024 + (ts + 1) * 512),
                        psum[bk][:, :]),
                      reads=[f"P{bk}"], writes=[f"acc{co}_{ts}"], tag=f"dE{tagp}_{ts}")
                else:
                    E("v", lambda e, co=co, ts=ts, bk=bk: e.tensor_tensor(
                        FB("acc", co * 1024 + ts * 512, co * 1024 + (ts + 1) * 512),
                        psum[bk][:, :],
                        FB("acc", co * 1024 + ts * 512, co * 1024 + (ts + 1) * 512),
                        ALU.add),
                      reads=[f"P{bk}", f"acc{co}_{ts}"], writes=[f"acc{co}_{ts}"],
                      tag=f"dE{tagp}_{ts}")

        def emit_wo_part(half, part, acc_first):
            for co in range(NCH):
                sl = co % 2
                DMA("g", BB(f"wob{sl}").rearrange("p (c f) -> p c f", c=8),
                    ap["wo"][co][:, part * 8:(part + 1) * 8, :], f"wo{sl}",
                    writes=[f"wob{sl}", "ustW"], tag=f"woL{half}_{part}_{co}")
                for ts in range(2):
                    bk = 4 + (co * 2 + ts) % 2
                    for cc_ in range(8):
                        ch = part * 8 + cc_
                        E("t", lambda e, cc_=cc_, ts=ts, bk=bk, sl=sl: e.matmul(
                            psum[bk][:, :], BB(f"wob{sl}", cc_ * 128, (cc_ + 1) * 128),
                            BB("rwkv", cc_ * HALF + ts * 512, cc_ * HALF + (ts + 1) * 512),
                            start=(cc_ == 0), stop=(cc_ == 7)),
                          reads=[f"wob{sl}", "ustW", f"rk{cc_}"], writes=[f"P{bk}"],
                          tag=f"wo{half}_{part}_{co}_{ts}_{cc_}")
                    if acc_first.pop((co, ts), False):
                        E("v", lambda e, co=co, ts=ts, bk=bk: e.tensor_copy(
                            FB("acc", co * 1024 + ts * 512, co * 1024 + (ts + 1) * 512),
                            psum[bk][:, :]),
                          reads=[f"P{bk}"], writes=[f"acc{co}_{ts}"],
                          tag=f"woE{half}_{part}_{co}_{ts}")
                    else:
                        E("v", lambda e, co=co, ts=ts, bk=bk: e.tensor_tensor(
                            FB("acc", co * 1024 + ts * 512, co * 1024 + (ts + 1) * 512),
                            psum[bk][:, :],
                            FB("acc", co * 1024 + ts * 512, co * 1024 + (ts + 1) * 512),
                            ALU.add),
                          reads=[f"P{bk}", f"acc{co}_{ts}"], writes=[f"acc{co}_{ts}"],
                          tag=f"woE{half}_{part}_{co}_{ts}")

        for half in range(2):
            hc0 = half * HALF
            acc_first = {(co, ts): True for co in range(NCH) for ts in range(2)}
            DMA("g", BB("xnh").rearrange("p (c f) -> p c f", c=NCH),
                scr_xn[half], "xnh", writes=["xnh"],
                extra_deps=(last_dma("sxn"),), tag=f"xnhL{half}")
            q = [(h, p) for h in range(8) for p in (1, 2, 3)]
            done = 0
            for fj in range(NF):
                fa = fj
                emit_gu(half, f"S{half}_{fa}", ap["sg"][fa], ap["su"][fa], fj, fj,
                        False)
                want = min(len(q), (24 * (fj + 1) + NF - 1) // NF)
                while done < want:
                    h, p = q[done]; done += 1
                    emit_corr(half, h, p)
            while done < len(q):
                h, p = q[done]; done += 1
                emit_corr(half, h, p)
            emit_wo_part(half, 0, acc_first)
            for co in range(NCH):
                emit_down(half, f"SD{half}_0_{co}", ap["sd"][co, 0], co, acc_first)
            q = [(h, p) for h in range(8, 16) for p in (1, 2, 3)]
            done = 0
            for fj in range(NF):
                fa = NF + fj
                emit_gu(half, f"S{half}_{fa}", ap["sg"][fa], ap["su"][fa], fj, fj,
                        False)
                want = min(len(q), (24 * (fj + 1) + NF - 1) // NF)
                while done < want:
                    h, p = q[done]; done += 1
                    emit_corr(half, h, p)
            while done < len(q):
                h, p = q[done]; done += 1
                emit_corr(half, h, p)
            emit_wo_part(half, 1, acc_first)
            for co in range(NCH):
                emit_down(half, f"SD{half}_1_{co}", ap["sd"][co, 1], co, acc_first)
            for n in range(NEXP):
                g, jj = n // 2, n % 2
                for tt in range(8):
                    bk = 6 + tt // 4
                    E("t", lambda e, tt=tt, g=g, jj=jj, bk=bk, half=half: e.matmul(
                        psum[bk][:, (tt % 4) * 128:(tt % 4) * 128 + 128],
                        FB("oneh", g * 128, (g + 1) * 128, p0=32 * jj, p1=32 * jj + 4),
                        FB("pk", (half * 8 + tt) * 128, (half * 8 + tt + 1) * 128,
                           p0=32 * jj, p1=32 * jj + 4),
                        start=True, stop=True),
                      reads=["oneh", "w0T" if jj == 0 else "w1T"], writes=[f"P{bk}"],
                      tag=f"wbM{half}_{n}_{tt}")
                    if tt % 4 == 3:
                        E("v", lambda e, tt=tt, bk=bk: e.tensor_copy(
                            FB("wb", (tt // 4) * 512, (tt // 4) * 512 + 512),
                            psum[bk][:, :]),
                          reads=[f"P{bk}"], writes=list(WBK), tag=f"wbE{half}_{n}_{tt}")
                for fj in range(NF):
                    emit_gu(half, f"E{half}_{n}_{fj}", ap["gw"][n, fj], ap["up"][n, fj],
                            fj, fj, True)
                for co in range(NCH):
                    emit_down(half, f"ED{half}_{n}_{co}", ap["dw"][n, co], co, acc_first)
            for tt in range(8):
                sl = tt % 2
                grow = hc0 + tt * 128
                xk = XR0 if sl == 0 else XR1
                DMA("y", FB(f"xres{sl}"), ap["x"][grow:grow + 128, :], f"xr{sl}",
                    writes=list(xk), tag=f"xr{half}_{tt}")
                for q in range(4):
                    bk = q % 2
                    for u in range(4):
                        co = q * 4 + u
                        E("t", lambda e, co=co, u=u, tt=tt, bk=bk: e.transpose(
                            psum[bk][:, u * 128:(u + 1) * 128],
                            FB("acc", co * 1024 + tt * 128, co * 1024 + (tt + 1) * 128),
                            FB("idf")),
                          reads=[f"acc{co}_{tt // 4}", "idf"], writes=[f"P{bk}"],
                          tag=f"oT{half}_{tt}_{co}")
                    E("v", lambda e, q=q, bk=bk, sl=sl: e.tensor_tensor(
                        FB("outtm", q * 512, (q + 1) * 512), psum[bk][:, :],
                        FB(f"xres{sl}", q * 512, (q + 1) * 512), ALU.add),
                      reads=[f"P{bk}"] + list(xk), writes=list(OTM),
                      tag=f"oE{half}_{tt}_{q}")
                DMA("y", out[grow:grow + 128, :], FB("outtm"), "outw", "full",
                    reads=list(OTM), tag=f"ow{half}_{tt}")
        P.join()

        return _emit(nc, P, sems_engine, sems_dma)


def _emit(nc, P, sems_engine, sems_dma):
    if DEBUG_TRUNC is not None:
        P.ops = P.ops[:DEBUG_TRUNC]
    P.finalize()
    engmap = {"t": "tensor", "v": "vector", "s": "scalar", "y": "sync",
              "g": "gpsimd"}
    with nc.Block() as blk:
        for eng in ("t", "v", "s", "y", "g"):
            def make(eng):
                def emitter(E_):
                    floor = {}
                    for o in P.ops:
                        if o["eng"] != eng:
                            continue
                        for sem, val in sorted(o["waits"].items()):
                            if val <= floor.get(sem, 0):
                                continue
                            h = (sems_engine[sem[1:]] if sem.startswith("E")
                                 else sems_dma[sem][0])
                            E_.wait_ge(h, val)
                            floor[sem] = val
                        if o["emit"] is None:
                            continue
                        inst = o["emit"](E_)
                        if o["dma"] is not None:
                            name, inc, mode, idx = o["dma"]
                            inst.then_inc(sems_dma[name][0], inc)
                        elif o["inc"]:
                            inst.then_inc(sems_engine[eng], 1)
                return emitter
            getattr(blk, engmap[eng])(make(eng))
    return nc


_CACHE = {}


def kernel(**inputs):
    d = _prep(inputs)
    if "nc" not in _CACHE:
        _CACHE["nc"] = _build()
    nc = _CACHE["nc"]
    in_maps = []
    for c in range(NCORE):
        in_maps.append({
            "x": d["x_slices"][c],
            "wk": d["wk"], "wv": d["wv"], "wr": d["wr"], "wo": d["wo"],
            "gw": d["gw"], "up": d["up"], "dw": d["dw"],
            "sg": d["sg"], "su": d["su"], "sd": d["sd"], "gwt": d["gwt"],
            "tdec": d["tdec"], "tf_eff": d["tf_eff"][c],
            "evec": d["evec"][c], "mvec": d["mvec"][c],
            "ident": d["ident"], "oneh": d["oneh"],
        })
    res = run_bass_kernel_spmd(nc, in_maps, core_ids=list(range(NCORE)))
    full = np.concatenate([res.results[c]["out"] for c in range(NCORE)], axis=0)
    return full.reshape(1, 16384, 2048)


# version 13
# speedup vs baseline: 1.0537x; 1.0537x over previous
import bisect
import contextlib
import numpy as np
import concourse.bass as bass
import concourse.mybir as mybir
from concourse.bass_utils import run_bass_kernel_spmd

F32 = mybir.dt.float32
BF16 = mybir.dt.bfloat16
AF = mybir.ActivationFunctionType
ALU = mybir.AluOpType
AX = mybir.AxisListType.X

NCORE = 8
TLOC = 2048
C = 2048
NCH = 16
NHEAD = 16
NEXP = 8
NF = 11
NFS = 22
HALF = 1024
E10 = float(np.float32(np.exp(10.0)))
EM10 = float(np.float32(np.exp(-10.0)))


class Prog:
    def __init__(self):
        self.ops = []
        self.lastw = {}
        self.readers = {}
        self.dma_cnt = {}

    def op(self, eng, emit, reads=(), writes=(), dma=None, tag="", extra_deps=(),
           small=False):
        oid = len(self.ops)
        deps = set(extra_deps)
        for k in reads:
            w = self.lastw.get(k)
            if w is not None:
                deps.add(w)
            self.readers.setdefault(k, []).append(oid)
        for k in writes:
            for r in self.readers.get(k, ()):
                deps.add(r)
            w = self.lastw.get(k)
            if w is not None:
                deps.add(w)
            self.readers[k] = []
            self.lastw[k] = oid
        deps.discard(oid)
        d = None
        if dma is not None:
            name, inc, mode = dma
            idx = self.dma_cnt.get(name, 0) + 1
            self.dma_cnt[name] = idx
            d = (name, inc, mode, idx)
        self.ops.append({"id": oid, "eng": eng, "emit": emit, "deps": deps,
                         "dma": d, "inc": False, "ord": None, "tag": tag,
                         "small": small})
        return oid

    def join(self):
        lasts = {}
        for o in self.ops:
            if o["emit"] is not None:
                lasts[o["eng"]] = o["id"]
        dma_marks = dict(self.dma_cnt)
        for eng in ("t", "v", "s", "y", "g"):
            oid = len(self.ops)
            deps = set(v for k, v in lasts.items() if k != eng)
            self.ops.append({"id": oid, "eng": eng, "emit": None, "deps": deps,
                             "dma": None, "inc": False, "ord": None,
                             "tag": "join", "dma_full": dma_marks})

    def finalize(self):
        sem_positions = {}
        for o in self.ops:
            if o["dma"] is not None:
                sem_positions.setdefault(o["dma"][0], []).append(o["id"])
        for o in self.ops:
            for d in o["deps"]:
                dep = self.ops[d]
                if dep["emit"] is None:
                    continue
                if dep["dma"] is None and (dep["eng"] != o["eng"]
                                           or dep.get("small")):
                    dep["inc"] = True
        cnt = {e: 0 for e in ("t", "v", "s", "y", "g")}
        for o in self.ops:
            if o["inc"]:
                cnt[o["eng"]] += 1
                o["ord"] = cnt[o["eng"]]
        for o in self.ops:
            waits = {}
            def addw(sem, val):
                if val > waits.get(sem, -1):
                    waits[sem] = val
            for d in o["deps"]:
                dep = self.ops[d]
                if dep["emit"] is None:
                    continue
                if dep["dma"] is not None:
                    name, inc, mode, idx = dep["dma"]
                    if mode == "full":
                        pos = sem_positions[name]
                        tot = bisect.bisect_left(pos, o["id"])
                        addw(name, inc * tot)
                    else:
                        addw(name, inc * idx)
                elif dep["eng"] != o["eng"] or dep.get("small"):
                    addw("E" + dep["eng"], dep["ord"])
            if "dma_full" in o:
                for name, tot in o["dma_full"].items():
                    waits_inc = 1 if name == "cc" else 16
                    addw(name, waits_inc * tot)
            o["waits"] = waits
        return cnt


def _prep(inputs):
    d = {}
    x = np.ascontiguousarray(np.asarray(inputs["L0_attn_x"], np.float32).reshape(16384, 2048))
    d["x_slices"] = [np.ascontiguousarray(x[c * TLOC:(c + 1) * TLOC]) for c in range(NCORE)]

    def t2(w):
        return np.ascontiguousarray(np.asarray(w, np.float32).T)

    def tile4(wt, kchunks, schunks):
        a = wt.reshape(kchunks, 128, schunks, 128)
        return np.ascontiguousarray(a.transpose(2, 1, 0, 3))

    d["wk"] = tile4(t2(inputs["L0_attn_Wk"]), NCH, NHEAD)
    d["wv"] = tile4(t2(inputs["L0_attn_Wv"]), NCH, NHEAD)
    d["wr"] = tile4(t2(inputs["L0_attn_Wr"]), NCH, NHEAD)
    d["wo"] = tile4(t2(inputs["L0_attn_Wo"]), NCH, NCH)

    gw = np.asarray(inputs["L0_ffn_gate_w"], np.float32)
    up = np.asarray(inputs["L0_ffn_up_w"], np.float32)
    dw = np.asarray(inputs["L0_ffn_down_w"], np.float32)
    d["gw"] = np.ascontiguousarray(
        gw.transpose(0, 2, 1).reshape(NEXP, NCH, 128, NF, 128).transpose(0, 3, 2, 1, 4))
    d["up"] = np.ascontiguousarray(
        up.transpose(0, 2, 1).reshape(NEXP, NCH, 128, NF, 128).transpose(0, 3, 2, 1, 4))
    d["dw"] = np.ascontiguousarray(
        dw.transpose(0, 2, 1).reshape(NEXP, NF, 128, NCH, 128).transpose(0, 3, 2, 1, 4))
    sg = t2(inputs["L0_ffn_sh_gate_w"])
    su = t2(inputs["L0_ffn_sh_up_w"])
    sd = t2(inputs["L0_ffn_sh_down_w"])
    d["sg"] = np.ascontiguousarray(sg.reshape(NCH, 128, NFS, 128).transpose(2, 1, 0, 3))
    d["su"] = np.ascontiguousarray(su.reshape(NCH, 128, NFS, 128).transpose(2, 1, 0, 3))
    sd4 = np.ascontiguousarray(sd.reshape(NFS, 128, NCH, 128).transpose(2, 1, 0, 3))
    d["sd"] = np.ascontiguousarray(
        sd4.reshape(NCH, 128, 2, NF, 128).transpose(0, 2, 1, 3, 4))
    gwt = t2(inputs["L0_ffn_gate_weight"])
    d["gwt"] = np.ascontiguousarray(gwt.reshape(NCH, 128, 8).transpose(1, 0, 2))

    d["tdec"] = np.ascontiguousarray(np.asarray(inputs["L0_attn_time_decay"], np.float32))
    tf = np.ascontiguousarray(np.asarray(inputs["L0_attn_time_first"], np.float32))
    z16 = np.zeros_like(tf)
    d["tf_eff"] = [tf if c == 0 else z16 for c in range(NCORE)]

    ev, mv = [], []
    for c in range(NCORE):
        e = np.zeros((1, 8), np.float32)
        m = np.zeros((1, 8), np.float32)
        for r in range(c):
            e[0, r] = float(TLOC * (c - 1 - r))
            m[0, r] = 1.0
        ev.append(e); mv.append(m)
    d["evec"] = ev; d["mvec"] = mv
    d["ident"] = np.eye(128, dtype=np.float32)
    oneh = np.zeros((4, 512), np.float32)
    for g in range(4):
        oneh[g, g * 128:(g + 1) * 128] = 1.0
    d["oneh"] = oneh
    return d


DEBUG_STAGE = None
DEBUG_NJT = 16
DEBUG_TRUNC = None


def _build():
    nc = bass.Bass("TRN2", target_bir_lowering=False)
    ap = {}

    def din(name, shape, dt=F32):
        ap[name] = nc.dram_tensor(name, shape, dt, kind="ExternalInput").ap()
    din("x", [TLOC, C])
    din("wk", [NHEAD, 128, NCH, 128]); din("wv", [NHEAD, 128, NCH, 128])
    din("wr", [NHEAD, 128, NCH, 128]); din("wo", [NCH, 128, NCH, 128])
    din("gw", [NEXP, NF, 128, NCH, 128]); din("up", [NEXP, NF, 128, NCH, 128])
    din("dw", [NEXP, NCH, 128, NF, 128])
    din("sg", [NFS, 128, NCH, 128]); din("su", [NFS, 128, NCH, 128])
    din("sd", [NCH, 2, 128, NF, 128])
    din("gwt", [128, NCH, 8])
    din("tdec", [16, 128]); din("tf_eff", [16, 128])
    din("evec", [1, 8]); din("mvec", [1, 8])
    din("ident", [128, 128]); din("oneh", [4, 512])
    out = nc.dram_tensor("out", [TLOC, C], F32, kind="ExternalOutput").ap()

    dbg = DEBUG_STAGE
    skind = "ExternalOutput" if dbg in ("s1", "s2", "s3") else Ellipsis
    if dbg is not None:
        dbg_xn = nc.dram_tensor("dbg_xn", [128, NCH * 2048], BF16, kind="ExternalOutput").ap()
        dbg_pk = nc.dram_tensor("dbg_pk", [128, 2048], F32, kind="ExternalOutput").ap()
    if dbg in ("s3",):
        dbg_c = nc.dram_tensor("dbg_c", [128, 384], F32, kind="ExternalOutput").ap()
    scr_xn = nc.dram_tensor("scr_xn", [2, 128, NCH, HALF], BF16).ap()
    sk = dict(kind="ExternalOutput") if dbg in ("s2", "s3") else {}
    scr_num = nc.dram_tensor("scr_num", [NHEAD, 128, TLOC], F32, **sk).ap()
    scr_den = nc.dram_tensor("scr_den", [NHEAD, 128, TLOC], F32, **sk).ap()
    scr_sr = nc.dram_tensor("scr_sr", [NHEAD, 128, TLOC], F32, **sk).ap()
    scr_pow = nc.dram_tensor("scr_pow", [NHEAD, 128, TLOC], F32).ap()
    contrib = nc.dram_tensor("contrib", [32, 128], F32).ap()
    gathered = nc.dram_tensor("gathered", [256, 128], F32).ap()

    P = Prog()

    fmap = {}
    _off = [0]
    def fal(name, cols, base=None):
        o = _off[0] if base is None else base
        fmap[name] = (o, cols)
        if base is None:
            _off[0] += cols
    fal("ones", 2048); fal("iota", 1024); fal("idf", 128); fal("pk", 2048)
    fal("gwt", 128); fal("oneh", 512); fal("csm", 448)
    fal("w0all", 64); fal("w1all", 64); fal("a2t", 128)
    G_F32 = _off[0]
    fal("xtm0", 2048, G_F32); fal("xtm1", 2048, G_F32 + 2048)
    fal("xntm0", 2048, G_F32 + 4096); fal("xntm1", 2048, G_F32 + 6144)
    fal("xnf0", 128, G_F32 + 8192); fal("xnf1", 128, G_F32 + 8320)
    fal("xnfF", 2048, G_F32 + 10944)
    fal("dump", 2048, G_F32 + 8448); fal("lns", 64, G_F32 + 10496)
    fal("rsm", 128, G_F32 + 10560)
    szA1 = 12992
    fal("w0", 1024, G_F32); fal("w1", 1024, G_F32 + 1024)
    fal("wv0", 1024, G_F32 + 2048); fal("wv1", 1024, G_F32 + 3072)
    fal("db", 2048, G_F32 + 4096)
    fal("num", 2048, G_F32 + 6144); fal("den", 2048, G_F32 + 8192)
    fal("sr", 2048, G_F32 + 10240); fal("pw2", 2048, G_F32 + 12288)
    szA2 = 14336
    fal("numh", 1024, G_F32); fal("denh", 1024, G_F32 + 1024)
    fal("srh", 1024, G_F32 + 2048); fal("powrec", 1024, G_F32 + 3072)
    fal("obuf", 2048, G_F32 + 4096)
    fal("acc", 16 * 1024, G_F32 + 6144)
    fal("wb", 1024, G_F32)
    szB = 6144 + 16384
    fal("xres0", 2048, G_F32)
    fal("xres1", 2048, G_F32 + 2048)
    fal("outtm", 2048, G_F32 + 4096)
    F32_COLS = G_F32 + max(szA1, szA2, szB)

    bmap = {}
    bmap["xnbf"] = (0, NCH * 2048)
    bo = NCH * 2048
    for nm in ("wkb0", "wkb1", "wvb0", "wvb1", "wrb0", "wrb1"):
        bmap[nm] = (bo, 2048); bo += 2048
    BF_A = bo
    bmap["xnh"] = (0, NCH * HALF)
    r0 = NCH * HALF
    bmap["rwkv"] = (r0, 8 * HALF)
    r1 = r0 + 8 * HALF
    bmap["hp"] = (r1, NF * HALF)
    r2 = r1 + NF * HALF
    bmap["gst0"] = (r2, 2048); bmap["gst1"] = (r2 + 2048, 2048)
    bmap["ust0"] = (r2 + 4096, 2048)
    bmap["dst0"] = (r2 + 6144, NF * 128); bmap["dst1"] = (r2 + 6144 + NF * 128, NF * 128)
    bmap["wob0"] = (r2 + 6144 + 2 * NF * 128, 1024)
    bmap["wob1"] = (r2 + 7168 + 2 * NF * 128, 1024)
    bmap["ustW"] = (r2 + 6144 + 2 * NF * 128, 2048)
    BF_B = r2 + 8192 + 2 * NF * 128
    BF_COLS = max(BF_A, BF_B)
    bpp = F32_COLS * 4 + BF_COLS * 2
    assert bpp <= 207 * 1024, f"SBUF over budget: {bpp} B/partition"

    st = contextlib.ExitStack()
    sems_engine = {}
    sems_dma = {}
    with st:
        AF32 = st.enter_context(nc.sbuf_tensor("AF32", [128, F32_COLS], F32))
        ABF = st.enter_context(nc.sbuf_tensor("ABF", [128, BF_COLS], BF16))
        psum = [st.enter_context(nc.psum_tensor(f"pb{i}", [128, 512], F32))
                for i in range(8)]

        def FB(name, a=0, b=None, p0=0, p1=128):
            o, n = fmap[name]
            if b is None:
                b = n
            return AF32[p0:p1, o + a:o + b]

        def BB(name, a=0, b=None):
            o, n = bmap[name]
            if b is None:
                b = n
            return ABF[:, o + a:o + b]

        for e in ("t", "v", "s", "g"):
            sems_engine[e] = st.enter_context(nc.semaphore(f"se_{e}"))
        for name in ("ldc", "xt0", "xt1", "wk0", "wk1", "wv0", "wv1", "wr0", "wr1",
                     "sxn", "snum", "sden", "ssr", "cwr", "cc", "grd",
                     "xnh", "nh0", "nh1", "dh0", "dh1", "sh0", "sh1", "spow", "ph0", "ph1",
                     "wo0", "wo1", "gs0", "gs1", "us0", "us1", "dsd", "ds1",
                     "xr0", "xr1", "outw"):
            inc = 1 if name == "cc" else 16
            sems_dma[name] = (st.enter_context(nc.semaphore(f"sd_{name}")), inc)

        def E(eng, fn, reads=(), writes=(), tag="", extra_deps=(), small=False):
            return P.op(eng, fn, reads=reads, writes=writes, tag=tag,
                        extra_deps=extra_deps, small=small)

        def DMA(queue, dst, src, sem, mode="slot", reads=(), writes=(), tag="",
                extra_deps=()):
            def fn(e, dst=dst, src=src):
                return e.dma_start(out=dst, in_=src)
            return P.op(queue, fn, reads=reads, writes=writes,
                        dma=(sem, sems_dma[sem][1], mode), tag=tag,
                        extra_deps=extra_deps)

        def last_dma(name):
            for o in reversed(P.ops):
                if o["dma"] is not None and o["dma"][0] == name:
                    return o["id"]
            raise KeyError(name)

        LAM = ("csm", 0, 16); DEC = ("csm", 16, 32); FT = ("csm", 32, 48)
        A2C = ("csm", 48, 80); CAN = ("csm", 80, 96); CAD = ("csm", 96, 112)
        EVE = ("csm", 112, 120); MVE = ("csm", 120, 128)
        TDR = ("csm", 128, 256)
        NL1 = ("csm", 256, 272)
        EBB = ("csm", 272, 280); MBB = ("csm", 280, 288)
        GTMP = ("csm", 288, 304)
        GT2 = ("csm", 304, 320)
        FRT = ("csm", 320, 336); TRT = ("csm", 336, 352)
        FR2 = ("csm", 352, 368)
        CANB = ("csm", 368, 384); CADB = ("csm", 384, 400)
        DP24 = ("csm", 400, 416)

        DMA("y", FB("idf"), ap["ident"], "ldc", "full", writes=["idf"])
        DMA("y", FB("gwt"), ap["gwt"].rearrange("p c e -> p (c e)"), "ldc", "full",
            writes=["gwt"])
        DMA("y", FB("oneh", p0=0, p1=4), ap["oneh"], "ldc", "full", writes=["oneh"])
        DMA("y", FB("oneh", p0=32, p1=36), ap["oneh"], "ldc", "full", writes=["oneh"])
        DMA("y", FB(*EVE, p0=0, p1=1), ap["evec"], "ldc", "full", writes=["ev"])
        DMA("y", FB(*MVE, p0=0, p1=1), ap["mvec"], "ldc", "full", writes=["mv"])
        DMA("y", FB(*TDR, p0=0, p1=16), ap["tdec"], "ldc", "full", writes=["tdraw"])
        E("t", lambda e: e.transpose(psum[0][0:128, 0:16], FB(*TDR, p0=0, p1=16),
                                     FB("idf", 0, 16, p0=0, p1=16)),
          reads=["tdraw", "idf"], writes=["P0"], tag="tdT")
        E("v", lambda e: e.tensor_copy(FB(*LAM), psum[0][:, 0:16]),
          reads=["P0"], writes=["lamr"], small=True)
        DMA("y", FB(*TDR, p0=0, p1=16), ap["tf_eff"], "ldc", "full",
            writes=["tdraw"], tag="tfL")
        E("t", lambda e: e.transpose(psum[1][0:128, 0:16], FB(*TDR, p0=0, p1=16),
                                     FB("idf", 0, 16, p0=0, p1=16)),
          reads=["tdraw", "idf"], writes=["P1"], tag="tfT")
        E("v", lambda e: e.tensor_copy(FB(*FT), psum[1][:, 0:16]),
          reads=["P1"], writes=["ftr"], small=True)
        E("s", lambda e: e.activation(FB(*LAM), FB(*LAM), AF.Exp),
          reads=["lamr"], writes=["lam"], small=True)
        E("s", lambda e: e.activation(FB(*DEC), FB(*LAM), AF.Exp, scale=-1.0),
          reads=["lam"], writes=["dec"], small=True)
        E("s", lambda e: e.activation(FB(*FT), FB(*FT), AF.Exp),
          reads=["ftr"], writes=["ft"], small=True)
        E("v", lambda e: e.memset(FB("ones"), 1.0), writes=["ones"])
        E("v", lambda e: e.memset(FB("pk"), 0.0), writes=["logits", "w0T", "w1T"])
        E("v", lambda e: e.memset(FB("iota"), 0.0), writes=["zer"])

        if dbg == "s0":
            DMA("y", dbg_pk[:, 0:384], AF32[:, fmap["csm"][0]:fmap["csm"][0] + 384], "outw",
                "full", reads=["lam", "dec", "ft", "iota"], tag="dbgc0")
            P.join()
            return _emit(nc, P, sems_engine, sems_dma)

        def LOGITS(a=0, b=2048):
            return FB("pk", a, b, p0=64, p1=72)
        NJ = DEBUG_NJT if dbg in ("s1a", "s1z") else NCH
        E("v", lambda e: e.memset(FB("lns", 16, 32), 0.0), writes=["ssq"])
        for j in range(NJ):
            sl = j % 2
            DMA("y", FB(f"xtm{sl}"), ap["x"][j * 128:(j + 1) * 128, :], f"xt{sl}",
                writes=[f"xtm{sl}"], tag=f"xLa{j}")
            E("v", lambda e, sl=sl, j=j: e.reduce_sum(FB("lns", j, j + 1),
                                                      FB(f"xtm{sl}"), axis=AX),
              reads=[f"xtm{sl}"], writes=["msum"], small=(j == NJ - 1), tag=f"rs{j}")
            E("s", lambda e, sl=sl, j=j: e.activation(
                FB("dump"), FB(f"xtm{sl}"), AF.Square, accum_out=FB("lns", 16 + j, 17 + j)),
              reads=[f"xtm{sl}", "ssq"], writes=["dump", "ssq"],
              small=(j == NJ - 1), tag=f"sq{j}")
        E("v", lambda e: e.tensor_scalar(FB("rsm", 0, 16), FB("lns", 0, 16),
                                         1.0 / 2048, None, ALU.mult),
          reads=["msum"], writes=["m"], small=True)
        E("v", lambda e: e.tensor_tensor(FB("rsm", 16, 32), FB("rsm", 0, 16),
                                         FB("rsm", 0, 16), ALU.mult),
          reads=["m"], writes=["m2"], small=True)
        E("v", lambda e: e.tensor_scalar(FB("rsm", 32, 48), FB("rsm", 16, 32),
                                         -1.0, None, ALU.mult),
          reads=["m2"], writes=["nm2"], small=True)
        E("v", lambda e: e.tensor_scalar(FB("rsm", 48, 64), FB("lns", 16, 32),
                                         1.0 / 2048, None, ALU.mult),
          reads=["ssq"], writes=["ex2"], small=True)
        E("v", lambda e: e.tensor_tensor(FB("rsm", 64, 80), FB("rsm", 48, 64),
                                         FB("rsm", 32, 48), ALU.add),
          reads=["ex2", "nm2"], writes=["varb"], small=True)
        E("v", lambda e: e.tensor_scalar(FB("rsm", 80, 96), FB("rsm", 64, 80),
                                         1e-5, None, ALU.add),
          reads=["varb"], writes=["vare"], small=True)
        E("s", lambda e: e.activation(FB("rsm", 96, 112), FB("rsm", 80, 96), AF.Ln),
          reads=["vare"], writes=["lnv"], small=True)
        E("s", lambda e: e.activation(FB("lns", 32, 48), FB("rsm", 96, 112), AF.Exp,
                                      scale=-0.5),
          reads=["lnv"], writes=["rstd"], small=True)
        E("v", lambda e: e.tensor_scalar(FB("rsm", 112, 128), FB("rsm", 0, 16),
                                         -1.0, None, ALU.mult),
          reads=["m"], writes=["negm"], small=True)
        E("v", lambda e: e.tensor_tensor(FB("lns", 48, 64), FB("rsm", 112, 128),
                                         FB("lns", 32, 48), ALU.mult),
          reads=["negm", "rstd"], writes=["nmr"], small=True)
        for j in range(NJ):
            sl = j % 2
            DMA("y", FB(f"xtm{sl}"), ap["x"][j * 128:(j + 1) * 128, :], f"xt{sl}",
                writes=[f"xtm{sl}"], tag=f"xLb{j}")
            E("v", lambda e, sl=sl, j=j: e.tensor_scalar(
                FB(f"xntm{sl}"), FB(f"xtm{sl}"), FB("lns", 32 + j, 33 + j),
                FB("lns", 48 + j, 49 + j), ALU.mult, ALU.add),
              reads=[f"xtm{sl}", "rstd", "nmr"], writes=[f"xntm{sl}"])
            for i in range(NCH):
                bk = i % 4
                E("t", lambda e, i=i, bk=bk, sl=sl: e.transpose(
                    psum[bk][:, 0:128], FB(f"xntm{sl}", i * 128, (i + 1) * 128),
                    FB("idf")),
                  reads=[f"xntm{sl}", "idf"], writes=[f"P{bk}"], tag=f"xT{j}_{i}")
                E("s", lambda e, bk=bk, i=i: e.copy(
                    FB("xnfF", i * 128, (i + 1) * 128), psum[bk][:, 0:128]),
                  reads=[f"P{bk}"], writes=[f"xnfF{i}"])
                E("v", lambda e, i=i, j=j: e.tensor_copy(
                    BB("xnbf", i * 2048 + j * 128, i * 2048 + (j + 1) * 128),
                    FB("xnfF", i * 128, (i + 1) * 128)),
                  reads=[f"xnfF{i}"], writes=[f"xnbf{i}"])
            for i in range(NCH):
                E("t", lambda e, i=i: e.matmul(
                    psum[4][0:8, 0:128], FB("gwt", i * 8, (i + 1) * 8),
                    FB("xnfF", i * 128, (i + 1) * 128),
                    start=(i == 0), stop=(i == NCH - 1)),
                  reads=[f"xnfF{i}", "gwt"], writes=["P4"], tag=f"gate{j}_{i}")
            E("v", lambda e, j=j: e.tensor_copy(LOGITS(j * 128, (j + 1) * 128),
                                                psum[4][0:8, 0:128]),
              reads=["P4"], writes=["logits"])
        if dbg == "s1z":
            DMA("g", dbg_xn[:, 0:2048], FB("xtm0"), "outw", "full",
                reads=["xtm0"], tag="dz1")
            DMA("g", dbg_xn[:, 2048:4096], FB("xntm0"), "outw", "full",
                reads=["xntm0"], tag="dz2")
            DMA("g", dbg_xn[:, 4096:4160], FB("lns"), "outw", "full",
                reads=["m", "ssq", "varb", "rstd", "nmr"], tag="dz3")
            DMA("y", dbg_xn[:, 6144:8192], BB("xnbf", 0, 2048), "outw", "full",
                reads=["xnbf0"], tag="dz4")
            P.join()
            return _emit(nc, P, sems_engine, sems_dma)

        if dbg == "s1a":
            DMA("y", dbg_xn, BB("xnbf"), "outw", "full",
                reads=[f"xnbf{i}" for i in range(NCH)], tag="dbgxn")
            DMA("y", dbg_pk, AF32[:, fmap["pk"][0]:fmap["pk"][0] + 2048], "outw",
                "full", reads=["logits"], tag="dbgpk")
            P.join()
            return _emit(nc, P, sems_engine, sems_dma)

        for half in range(2):
            for i in range(NCH):
                DMA("y", scr_xn[half, :, i, :],
                    BB("xnbf", i * 2048 + half * HALF, i * 2048 + half * HALF + HALF),
                    "sxn", "full", reads=[f"xnbf{i}"], tag=f"sxnw{half}_{i}")

        def _dbg_dump_s1x():
            DMA("y", dbg_xn, BB("xnbf"), "outw", "full",
                reads=[f"xnbf{i}" for i in range(NCH)], tag="dbgxn")
            DMA("y", dbg_pk, AF32[:, fmap["pk"][0]:fmap["pk"][0] + 2048], "outw",
                "full", reads=["logits"], tag="dbgpk")
            P.join()

        if dbg == "s1b":
            _dbg_dump_s1x()
            return _emit(nc, P, sems_engine, sems_dma)

        def R(a, b):
            return FB("rsm", a, b)
        for j in range(NCH):
            E("t", lambda e, j=j: e.transpose(psum[5][0:128, 0:8],
                                              LOGITS(j * 128, (j + 1) * 128),
                                              FB("idf", 64, 72, p0=64, p1=72)),
              reads=["logits", "idf"], writes=["P5"], tag=f"lgT{j}")
            E("v", lambda e: e.tensor_copy(R(0, 8), psum[5][:, 0:8]),
              reads=["P5"], writes=["lgtm"], small=True)
            E("v", lambda e: e.reduce_max(R(8, 9), R(0, 8), axis=AX),
              reads=["lgtm"], writes=["mx"], small=True)
            E("v", lambda e: e.tensor_scalar(R(9, 10), R(8, 9), -1.0, None, ALU.mult),
              reads=["mx"], writes=["nmx"], small=True)
            E("v", lambda e: e.tensor_scalar(R(10, 18), R(0, 8), R(9, 10), None, ALU.add),
              reads=["lgtm", "nmx"], writes=["sh"], small=True)
            E("s", lambda e: e.activation(R(18, 26), R(10, 18), AF.Exp),
              reads=["sh"], writes=["exq"], small=True)
            E("v", lambda e: e.reduce_sum(R(26, 27), R(18, 26), axis=AX),
              reads=["exq"], writes=["ssum"], small=True)
            E("v", lambda e: e.reciprocal(R(27, 28), R(26, 27)),
              reads=["ssum"], writes=["rsum"], small=True)
            E("v", lambda e: e.tensor_scalar(R(28, 36), R(18, 26), R(27, 28), None,
                                             ALU.mult),
              reads=["exq", "rsum"], writes=["sc8"], small=True)
            e8 = R(28, 36).rearrange("p (g t) -> p g t", t=2)
            E("v", lambda e, e8=e8: e.tensor_tensor(R(36, 40), e8[:, :, 0],
                                                    e8[:, :, 1], ALU.max),
              reads=["sc8"], writes=["g4"], small=True)
            E("v", lambda e: e.reduce_max(R(40, 41), R(36, 40), axis=AX),
              reads=["g4"], writes=["gm1"], small=True)
            E("v", lambda e: e.tensor_scalar(R(41, 45), R(36, 40), R(40, 41), None,
                                             ALU.is_ge),
              reads=["g4", "gm1"], writes=["i1"], small=True)
            E("v", lambda e: e.scalar_tensor_tensor(R(45, 49), R(41, 45), -1e30,
                                                    R(36, 40), ALU.mult, ALU.add),
              reads=["i1", "g4"], writes=["gneg"], small=True)
            E("v", lambda e: e.reduce_max(R(49, 50), R(45, 49), axis=AX),
              reads=["gneg"], writes=["gm2"], small=True)
            E("v", lambda e: e.tensor_scalar(R(50, 54), R(36, 40), R(49, 50), None,
                                             ALU.is_ge),
              reads=["g4", "gm2"], writes=["gmask"], small=True)
            E("v", lambda e, e8=e8: e.tensor_tensor(R(54, 58), e8[:, :, 0],
                                                    R(50, 54), ALU.mult),
              reads=["sc8", "gmask"], writes=["m0"], small=True)
            E("v", lambda e, e8=e8: e.tensor_tensor(R(58, 62), e8[:, :, 1],
                                                    R(50, 54), ALU.mult),
              reads=["sc8", "gmask"], writes=["m1"], small=True)
            E("v", lambda e: e.reduce_max(R(62, 63), R(54, 58), axis=AX),
              reads=["m0"], writes=["t1a"], small=True)
            E("v", lambda e: e.reduce_max(R(63, 64), R(58, 62), axis=AX),
              reads=["m1"], writes=["t1b"], small=True)
            E("v", lambda e: e.tensor_tensor(R(64, 65), R(62, 63), R(63, 64), ALU.max),
              reads=["t1a", "t1b"], writes=["t1"], small=True)
            E("v", lambda e: e.tensor_scalar(R(65, 69), R(54, 58), R(64, 65), None,
                                             ALU.is_ge),
              reads=["m0", "t1"], writes=["k0a"], small=True)
            E("v", lambda e: e.scalar_tensor_tensor(R(69, 73), R(65, 69), -1e30,
                                                    R(54, 58), ALU.mult, ALU.add),
              reads=["k0a", "m0"], writes=["k0n"], small=True)
            E("v", lambda e: e.tensor_scalar(R(73, 77), R(58, 62), R(64, 65), None,
                                             ALU.is_ge),
              reads=["m1", "t1"], writes=["k1a"], small=True)
            E("v", lambda e: e.scalar_tensor_tensor(R(77, 81), R(73, 77), -1e30,
                                                    R(58, 62), ALU.mult, ALU.add),
              reads=["k1a", "m1"], writes=["k1n"], small=True)
            E("v", lambda e: e.reduce_max(R(81, 82), R(69, 73), axis=AX),
              reads=["k0n"], writes=["t2a"], small=True)
            E("v", lambda e: e.reduce_max(R(82, 83), R(77, 81), axis=AX),
              reads=["k1n"], writes=["t2b"], small=True)
            E("v", lambda e: e.tensor_tensor(R(83, 84), R(81, 82), R(82, 83), ALU.max),
              reads=["t2a", "t2b"], writes=["t2"], small=True)
            E("v", lambda e: e.tensor_scalar(R(84, 88), R(54, 58), R(83, 84), None,
                                             ALU.is_ge),
              reads=["m0", "t2"], writes=["k0"], small=True)
            E("v", lambda e: e.tensor_scalar(R(88, 92), R(58, 62), R(83, 84), None,
                                             ALU.is_ge),
              reads=["m1", "t2"], writes=["k1"], small=True)
            E("v", lambda e, j=j: e.tensor_tensor(FB("w0all", j * 4, (j + 1) * 4),
                                                  R(54, 58), R(84, 88), ALU.mult),
              reads=["m0", "k0"], writes=["w0all"], small=True)
            E("v", lambda e, j=j: e.tensor_tensor(FB("w1all", j * 4, (j + 1) * 4),
                                                  R(58, 62), R(88, 92), ALU.mult),
              reads=["m1", "k1"], writes=["w1all"], small=True)
        for j in range(NCH):
            E("t", lambda e, j=j: e.transpose(psum[6][0:4, 0:128],
                                              FB("w0all", j * 4, (j + 1) * 4),
                                              FB("idf")),
              reads=["w0all", "idf"], writes=["P6"], tag=f"w0T{j}")
            E("v", lambda e, j=j: e.tensor_copy(
                FB("pk", j * 128, (j + 1) * 128, p0=0, p1=4), psum[6][0:4, 0:128]),
              reads=["P6"], writes=["w0T"])
            E("t", lambda e, j=j: e.transpose(psum[7][0:4, 0:128],
                                              FB("w1all", j * 4, (j + 1) * 4),
                                              FB("idf")),
              reads=["w1all", "idf"], writes=["P7"], tag=f"w1T{j}")
            E("v", lambda e, j=j: e.tensor_copy(
                FB("pk", j * 128, (j + 1) * 128, p0=32, p1=36), psum[7][0:4, 0:128]),
              reads=["P7"], writes=["w1T"])

        P.join()

        def _dbg_dump_s1():
            DMA("y", dbg_xn, BB("xnbf"), "outw", "full",
                reads=[f"xnbf{i}" for i in range(NCH)], tag="dbgxn")
            DMA("y", dbg_pk, AF32[:, fmap["pk"][0]:fmap["pk"][0] + 2048], "outw",
                "full", reads=["logits", "w0T", "w1T"], tag="dbgpk")
            P.join()

        if dbg == "s1":
            _dbg_dump_s1()
            return _emit(nc, P, sems_engine, sems_dma)

        def wload(h, sl):
            for nm in ("wk", "wv", "wr"):
                DMA("g", BB(f"{nm}b{sl}").rearrange("p (c f) -> p c f", c=NCH),
                    ap[nm][h], f"{nm}{sl}", writes=[f"{nm}b{sl}"], tag=f"{nm}L{h}")
        wload(0, 0)
        wload(1, 1)
        for h in range(NHEAD):
            sl = h % 2
            E("v", lambda e, h=h: e.tensor_scalar(FB("db"), FB("ones"),
                                                  FB(*DEC)[:, h:h + 1], None, ALU.mult),
              reads=["ones", "dec"], writes=["db"], tag=f"db{h}")
            for ts in range(4):
                wsl = FB(f"w{ts // 2}", (ts % 2) * 512, (ts % 2) * 512 + 512)
                wvsl = FB(f"wv{ts // 2}", (ts % 2) * 512, (ts % 2) * 512 + 512)
                bk, bv, br = ts % 2, 2 + ts % 2, 4 + ts % 2
                for i in range(NCH):
                    E("t", lambda e, i=i, ts=ts, bk=bk, sl=sl: e.matmul(
                        psum[bk][:, :], BB(f"wkb{sl}", i * 128, (i + 1) * 128),
                        BB("xnbf", i * 2048 + ts * 512, i * 2048 + (ts + 1) * 512),
                        start=(i == 0), stop=(i == NCH - 1)),
                      reads=[f"wkb{sl}", f"xnbf{i}"], writes=[f"P{bk}"],
                      tag=f"k{h}_{ts}_{i}")
                E("s", lambda e, wsl=wsl, bk=bk: e.activation(wsl, psum[bk][:, :], AF.Exp),
                  reads=[f"P{bk}"], writes=[f"wq{ts}"], tag=f"wexp{h}_{ts}")
                E("v", lambda e, wsl=wsl: e.tensor_scalar(wsl, wsl, E10, EM10,
                                                          ALU.min, ALU.max),
                  reads=[f"wq{ts}"], writes=[f"wq{ts}"], tag=f"wclip{h}_{ts}")
                if ts == 0:
                    E("v", lambda e, h=h: e.tensor_tensor(
                        FB(*NL1)[:, 2:3], FB("w0", 0, 1), FB(*FT)[:, h:h + 1],
                        ALU.mult),
                      reads=["wq0", "ft"], writes=["fm"], small=True,
                      tag=f"fm{h}")
                    E("v", lambda e: e.tensor_copy(FB("w0", 0, 1), FB(*NL1)[:, 2:3]),
                      reads=["fm"], writes=["wq0"], small=True, tag=f"fmc{h}")
                for i in range(NCH):
                    E("t", lambda e, i=i, ts=ts, bv=bv, sl=sl: e.matmul(
                        psum[bv][:, :], BB(f"wvb{sl}", i * 128, (i + 1) * 128),
                        BB("xnbf", i * 2048 + ts * 512, i * 2048 + (ts + 1) * 512),
                        start=(i == 0), stop=(i == NCH - 1)),
                      reads=[f"wvb{sl}", f"xnbf{i}"], writes=[f"P{bv}"],
                      tag=f"v{h}_{ts}_{i}")
                E("v", lambda e, wsl=wsl, wvsl=wvsl, bv=bv: e.tensor_tensor(
                    wvsl, psum[bv][:, :], wsl, ALU.mult),
                  reads=[f"P{bv}", f"wq{ts}"], writes=[f"wvq{ts}"], tag=f"wv{h}_{ts}")
                for i in range(NCH):
                    E("t", lambda e, i=i, ts=ts, br=br, sl=sl: e.matmul(
                        psum[br][:, :], BB(f"wrb{sl}", i * 128, (i + 1) * 128),
                        BB("xnbf", i * 2048 + ts * 512, i * 2048 + (ts + 1) * 512),
                        start=(i == 0), stop=(i == NCH - 1)),
                      reads=[f"wrb{sl}", f"xnbf{i}"], writes=[f"P{br}"],
                      tag=f"r{h}_{ts}_{i}")
                E("s", lambda e, ts=ts, br=br: e.activation(
                    FB("sr", ts * 512, (ts + 1) * 512), psum[br][:, :], AF.Tanh,
                    scale=0.5),
                  reads=[f"P{br}"], writes=["srq"], tag=f"tanh{h}_{ts}")
            if h + 2 < NHEAD:
                wload(h + 2, (h + 2) % 2)
            E("v", lambda e: e.tensor_tensor_scan(
                FB("num", 0, 1024), FB("db", 0, 1024), FB("w0" if False else "wv0"),
                0.0, ALU.mult, ALU.add),
              reads=["db", "wvq0", "wvq1"], writes=["numq0"], small=True,
              tag=f"scN0_{h}")
            E("v", lambda e: e.tensor_tensor_scan(
                FB("num", 1024, 2048), FB("db", 1024, 2048), FB("wv1"),
                FB("num", 1023, 1024), ALU.mult, ALU.add),
              reads=["db", "wvq2", "wvq3", "numq0"], writes=["numq1"], small=True,
              tag=f"scN1_{h}")
            E("v", lambda e: e.tensor_tensor_scan(
                FB("den", 0, 1024), FB("db", 0, 1024), FB("w0"),
                0.0, ALU.mult, ALU.add),
              reads=["db", "wq0", "wq1"], writes=["denq0"], small=True,
              tag=f"scD0_{h}")
            E("v", lambda e: e.tensor_tensor_scan(
                FB("den", 1024, 2048), FB("db", 1024, 2048), FB("w1"),
                FB("den", 1023, 1024), ALU.mult, ALU.add),
              reads=["db", "wq2", "wq3", "denq0"], writes=["denq1"], small=True,
              tag=f"scD1_{h}")
            E("v", lambda e, h=h: e.tensor_copy(FB(*A2C)[:, h:h + 1],
                                                FB("num", 2047, 2048)),
              reads=["numq1"], writes=["A2"], small=True, tag=f"An{h}")
            E("v", lambda e, h=h: e.tensor_copy(FB(*A2C)[:, 16 + h:17 + h],
                                                FB("den", 2047, 2048)),
              reads=["denq1"], writes=["A2"], small=True, tag=f"Ad{h}")
            E("v", lambda e: e.tensor_scalar(FB("den"), FB("den"), 1e-6, None,
                                             ALU.add),
              reads=["denq0", "denq1"], writes=["denq0", "denq1"], tag=f"depsA{h}")
            E("v", lambda e: e.tensor_tensor_scan(
                FB("pw2", 0, 1024), FB("db", 0, 1024), FB("iota"),
                1.0, ALU.mult, ALU.add),
              reads=["db", "zer"], writes=["pwq0"], small=True, tag=f"scP0_{h}")
            E("v", lambda e: e.tensor_tensor_scan(
                FB("pw2", 1024, 2048), FB("db", 1024, 2048), FB("iota"),
                FB("pw2", 1023, 1024), ALU.mult, ALU.add),
              reads=["db", "zer", "pwq0"], writes=["pwq1"], small=True,
              tag=f"scP1_{h}")
            DMA("y", scr_pow[h], FB("pw2"), "spow", "full",
                reads=["pwq0", "pwq1"], tag=f"spow{h}")
            DMA("y", scr_num[h], FB("num"), "snum", "full",
                reads=["numq0", "numq1"], tag=f"snum{h}")
            DMA("y", scr_den[h], FB("den"), "sden", "full",
                reads=["denq0", "denq1"], tag=f"sden{h}")
            DMA("y", scr_sr[h], FB("sr"), "ssr", "full", reads=["srq"], tag=f"ssr{h}")

        P.join()

        E("t", lambda e: e.transpose(psum[6][0:32, 0:128], FB(*A2C), FB("idf")),
          reads=["A2", "idf"], writes=["P6"], tag="A2T")
        E("v", lambda e: e.tensor_copy(FB("a2t", 0, 128, p0=0, p1=32),
                                       psum[6][0:32, 0:128]),
          reads=["P6"], writes=["A2T"], small=True, tag="A2Tc")
        DMA("y", contrib, FB("a2t", 0, 128, p0=0, p1=32), "cwr", "full",
            reads=["A2T"], writes=["contribd"], tag="cwr")
        def ccfn(e):
            return e.collective_compute(
                "AllGather", ALU.bypass, replica_groups=[list(range(NCORE))],
                ins=[contrib.opt()], outs=[gathered.opt()])
        P.op("g", ccfn, reads=["contribd"], writes=["gatheredd"],
             dma=("cc", 1, "slot"), tag="allgather")
        E("t", lambda e: e.matmul(psum[7][:, 0:8], FB("ones", 0, 128, p0=0, p1=1),
                                  FB(*EVE, p0=0, p1=1), start=True, stop=True),
          reads=["ones", "ev"], writes=["P7"], tag="ebB")
        E("v", lambda e: e.tensor_copy(FB(*EBB), psum[7][:, 0:8]),
          reads=["P7"], writes=["ebb"], small=True, tag="ebc")
        E("t", lambda e: e.matmul(psum[7][:, 0:8], FB("ones", 0, 128, p0=0, p1=1),
                                  FB(*MVE, p0=0, p1=1), start=True, stop=True),
          reads=["ones", "mv"], writes=["P7"], tag="mbB")
        E("v", lambda e: e.tensor_copy(FB(*MBB), psum[7][:, 0:8]),
          reads=["P7"], writes=["mbb"], small=True, tag="mbc")
        E("v", lambda e: e.memset(FB(*CAN), 0.0), writes=["can0"], small=True)
        E("v", lambda e: e.memset(FB(*CAD), 0.0), writes=["cad0"], small=True)
        for r in range(NCORE - 1):
            pa = r % 2
            CNa, CNb = (CAN, CANB) if pa == 0 else (CANB, CAN)
            CDa, CDb = (CAD, CADB) if pa == 0 else (CADB, CAD)
            ka, kb = (f"can{r}", f"can{r + 1}"), None
            DMA("y", FB(*TDR, p0=0, p1=16), gathered[r * 32:r * 32 + 16, :],
                "grd", "full", reads=["gatheredd"], writes=["tdraw"], tag=f"gn{r}")
            E("t", lambda e: e.transpose(psum[6][0:128, 0:16], FB(*TDR, p0=0, p1=16),
                                         FB("idf", 0, 16, p0=0, p1=16)),
              reads=["tdraw", "idf"], writes=["P6"], tag=f"gnT{r}")
            E("v", lambda e: e.tensor_copy(FB(*GTMP), psum[6][:, 0:16]),
              reads=["P6"], writes=["anr"], small=True, tag=f"gnc{r}")
            DMA("y", FB(*TDR, p0=0, p1=16), gathered[r * 32 + 16:r * 32 + 32, :],
                "grd", "full", reads=["gatheredd"], writes=["tdraw"], tag=f"gd{r}")
            E("t", lambda e: e.transpose(psum[6][0:128, 0:16], FB(*TDR, p0=0, p1=16),
                                         FB("idf", 0, 16, p0=0, p1=16)),
              reads=["tdraw", "idf"], writes=["P6"], tag=f"gdT{r}")
            E("v", lambda e: e.tensor_copy(FB(*GT2), psum[6][:, 0:16]),
              reads=["P6"], writes=["adr"], small=True, tag=f"gdc{r}")
            E("v", lambda e, r=r: e.tensor_scalar(FB(*FRT), FB(*LAM),
                                                  FB(*EBB)[:, r:r + 1], None, ALU.mult),
              reads=["lam", "ebb"], writes=["fr"], small=True, tag=f"f1_{r}")
            E("s", lambda e: e.activation(FB(*FR2), FB(*FRT), AF.Exp, scale=-1.0),
              reads=["fr"], writes=["fr2"], small=True, tag=f"f2_{r}")
            E("v", lambda e, r=r: e.tensor_scalar(FB(*FRT), FB(*FR2),
                                                  FB(*MBB)[:, r:r + 1], None, ALU.mult),
              reads=["fr2", "mbb"], writes=["fr"], small=True, tag=f"f3_{r}")
            E("v", lambda e: e.tensor_tensor(FB(*TRT), FB(*FRT), FB(*GTMP), ALU.mult),
              reads=["fr", "anr"], writes=["tnr"], small=True, tag=f"f4_{r}")
            E("v", lambda e, CNa=CNa, CNb=CNb: e.tensor_tensor(
                FB(*CNb), FB(*CNa), FB(*TRT), ALU.add),
              reads=["tnr", f"can{r}"], writes=[f"can{r + 1}"], small=True,
              tag=f"f5_{r}")
            E("v", lambda e: e.tensor_tensor(FB(*TRT), FB(*FRT), FB(*GT2), ALU.mult),
              reads=["fr", "adr"], writes=["tdr"], small=True, tag=f"f6_{r}")
            E("v", lambda e, CDa=CDa, CDb=CDb: e.tensor_tensor(
                FB(*CDb), FB(*CDa), FB(*TRT), ALU.add),
              reads=["tdr", f"cad{r}"], writes=[f"cad{r + 1}"], small=True,
              tag=f"f7_{r}")
        if (NCORE - 1) % 2 == 1:
            E("v", lambda e: e.tensor_copy(FB(*CAN), FB(*CANB)),
              reads=[f"can{NCORE - 1}"], writes=["can"], small=True, tag="canf")
            E("v", lambda e: e.tensor_copy(FB(*CAD), FB(*CADB)),
              reads=[f"cad{NCORE - 1}"], writes=["cad"], small=True, tag="cadf")
        else:
            E("v", lambda e: e.tensor_copy(FB(*CANB), FB(*CAN)),
              reads=[f"can{NCORE - 1}"], writes=["can"], small=True, tag="canf")
            E("v", lambda e: e.tensor_copy(FB(*CADB), FB(*CAD)),
              reads=[f"cad{NCORE - 1}"], writes=["cad"], small=True, tag="cadf")

        if dbg == "s2":
            _dbg_dump_s1()
            return _emit(nc, P, sems_engine, sems_dma)
        if dbg == "s3":
            _dbg_dump_s1()
            DMA("y", dbg_c, AF32[:, fmap["csm"][0]:fmap["csm"][0] + 384], "outw",
                "full", reads=["can", "cad", "lam", "dec", "A2"], tag="dbgc")
            P.join()
            return _emit(nc, P, sems_engine, sems_dma)

        P.join()

        WBK = ("wb", "numh")
        PWB = lambda: FB("obuf", 1024, 2048)
        XR0 = ("xres0", "numh", "denh")
        XR1 = ("xres1", "srh", "pow")
        OTM = ("outtm", "gsil", "pwb")

        def emit_corr(half, h, piece=0):
            hc0 = half * HALF
            slot = h % 8
            sl = h % 2
            if piece in (0, 1):
                _corr_p1(half, h, hc0, sl)
            if piece in (0, 2):
                _corr_p2(half, h)
            if piece in (0, 3):
                _corr_p3(half, h, slot)

        def _corr_p1(half, h, hc0, sl):
            DMA("y", FB("numh"), scr_num[h][:, hc0:hc0 + HALF], f"nh{sl}",
                writes=["numh"], extra_deps=(last_dma("snum"),), tag=f"nhL{half}_{h}")
            DMA("y", FB("denh"), scr_den[h][:, hc0:hc0 + HALF], f"dh{sl}",
                writes=["denh"], extra_deps=(last_dma("sden"),), tag=f"dhL{half}_{h}")
            DMA("y", FB("srh"), scr_sr[h][:, hc0:hc0 + HALF], f"sh{sl}",
                writes=["srh"], extra_deps=(last_dma("ssr"),), tag=f"shL{half}_{h}")
            DMA("y", FB("powrec"), scr_pow[h][:, hc0:hc0 + HALF], f"ph{sl}",
                writes=["pow"], extra_deps=(last_dma("spow"),),
                tag=f"phL{half}_{h}")
            E("v", lambda e, h=h: e.scalar_tensor_tensor(
                PWB(), FB("powrec"), FB(*CAN)[:, h:h + 1], FB("numh"),
                ALU.mult, ALU.add),
              reads=["pow", "can", "numh"], writes=["pwb"], tag=f"cn{half}_{h}")
            E("v", lambda e, h=h: e.scalar_tensor_tensor(
                FB("numh"), FB("powrec"), FB(*CAD)[:, h:h + 1], FB("denh"),
                ALU.mult, ALU.add),
              reads=["pow", "cad", "denh"], writes=["numh"], tag=f"cd{half}_{h}")

        def _corr_p2(half, h):
            E("v", lambda e: e.reciprocal(FB("denh", 0, 512), FB("numh", 0, 512)),
              reads=["numh"], writes=["denh"], tag=f"recA{half}_{h}")
            E("v", lambda e: e.reciprocal(FB("denh", 512, 1024), FB("numh", 512, 1024)),
              reads=["numh"], writes=["denh"], tag=f"recB{half}_{h}")
            E("v", lambda e: e.tensor_tensor(FB("numh"), PWB(), FB("denh"),
                                             ALU.mult),
              reads=["pwb", "denh"], writes=["numh"], tag=f"wkv{half}_{h}")

        def _corr_p3(half, h, slot):
            E("g", lambda e: e.tensor_scalar(FB("powrec"), FB("srh"), 1.0, 0.5,
                                             ALU.add, ALU.mult),
              reads=["srh", "pow"], writes=["pow"], tag=f"sig{half}_{h}")
            E("v", lambda e, slot=slot: e.tensor_tensor(
                BB("rwkv", slot * HALF, (slot + 1) * HALF), FB("numh"), FB("powrec"),
                ALU.mult),
              reads=["numh", "pow"], writes=[f"rk{slot}"], tag=f"rk{half}_{h}")

        def emit_gu(half, tagp, gsrc, usrc, fj, hpj, wbmul):
            sl = fj % 2
            if wbmul and fj % 2 == 1:
                ubuf, usem, ukeys = "ustW", "us1", ["ustW", "wob0", "wob1"]
            else:
                ubuf, usem, ukeys = "ust0", "us0", ["ust0"]
            DMA("g", BB(f"gst{sl}").rearrange("p (c f) -> p c f", c=NCH),
                gsrc, f"gs{sl}", writes=[f"gst{sl}"], tag=f"gL{tagp}")
            DMA("g", BB(ubuf).rearrange("p (c f) -> p c f", c=NCH),
                usrc, usem, writes=ukeys, tag=f"uL{tagp}")
            bg = 2 * (fj % 2)
            bu = 6 - 2 * (fj % 2)
            for ts in range(2):
                for i in range(NCH):
                    E("t", lambda e, i=i, ts=ts, sl=sl, bg=bg: e.matmul(
                        psum[bg + ts][:, :], BB(f"gst{sl}", i * 128, (i + 1) * 128),
                        BB("xnh", i * HALF + ts * 512, i * HALF + (ts + 1) * 512),
                        start=(i == 0), stop=(i == NCH - 1)),
                      reads=[f"gst{sl}", "xnh"], writes=[f"P{bg + ts}"],
                      tag=f"g{tagp}_{ts}_{i}")
                E("s", lambda e, ts=ts, bg=bg: e.activation(
                    FB("obuf", ts * 512, (ts + 1) * 512), psum[bg + ts][:, :], AF.Silu),
                  reads=[f"P{bg + ts}"], writes=["gsil"], tag=f"sil{tagp}_{ts}")
                for i in range(NCH):
                    E("t", lambda e, i=i, ts=ts, ubuf=ubuf, bu=bu: e.matmul(
                        psum[bu + ts][:, :], BB(ubuf, i * 128, (i + 1) * 128),
                        BB("xnh", i * HALF + ts * 512, i * HALF + (ts + 1) * 512),
                        start=(i == 0), stop=(i == NCH - 1)),
                      reads=list(ukeys) + ["xnh"], writes=[f"P{bu + ts}"],
                      tag=f"u{tagp}_{ts}_{i}")
                if wbmul:
                    E("v", lambda e, ts=ts, bu=bu: e.tensor_tensor(
                        FB("obuf", ts * 512, (ts + 1) * 512), psum[bu + ts][:, :],
                        FB("obuf", ts * 512, (ts + 1) * 512), ALU.mult),
                      reads=[f"P{bu + ts}", "gsil"], writes=["gsil"],
                      tag=f"gu{tagp}_{ts}")
                else:
                    E("v", lambda e, ts=ts, hpj=hpj, bu=bu: e.tensor_tensor(
                        BB("hp", hpj * HALF + ts * 512, hpj * HALF + (ts + 1) * 512),
                        psum[bu + ts][:, :], FB("obuf", ts * 512, (ts + 1) * 512),
                        ALU.mult),
                      reads=[f"P{bu + ts}", "gsil"], writes=[f"hp{hpj}"],
                      tag=f"gu{tagp}_{ts}")
            if wbmul:
                E("v", lambda e, hpj=hpj: e.tensor_tensor(
                    BB("hp", hpj * HALF, (hpj + 1) * HALF), FB("obuf", 0, 1024),
                    FB("wb", 0, 1024), ALU.mult),
                  reads=["gsil"] + list(WBK), writes=[f"hp{hpj}"], tag=f"hp{tagp}")

        def emit_down(half, tagp, dsrc, co, acc_first):
            sl = co % 2
            DMA("g", BB(f"dst{sl}").rearrange("p (c f) -> p c f", c=NF),
                dsrc, "dsd" if sl == 0 else "ds1", writes=[f"dst{sl}"],
                tag=f"dL{tagp}")
            for ts in range(2):
                bk = 6 + ts
                for fj in range(NF):
                    E("t", lambda e, fj=fj, ts=ts, bk=bk, sl=sl: e.matmul(
                        psum[bk][:, :], BB(f"dst{sl}", fj * 128, (fj + 1) * 128),
                        BB("hp", fj * HALF + ts * 512, fj * HALF + (ts + 1) * 512),
                        start=(fj == 0), stop=(fj == NF - 1)),
                      reads=[f"dst{sl}", f"hp{fj}"], writes=[f"P{bk}"],
                      tag=f"d{tagp}_{ts}_{fj}")
                if acc_first.pop((co, ts), False):
                    E("v", lambda e, co=co, ts=ts, bk=bk: e.tensor_copy(
                        FB("acc", co * 1024 + ts * 512, co * 1024 + (ts + 1) * 512),
                        psum[bk][:, :]),
                      reads=[f"P{bk}"], writes=[f"acc{co}_{ts}"], tag=f"dE{tagp}_{ts}")
                else:
                    E("v", lambda e, co=co, ts=ts, bk=bk: e.tensor_tensor(
                        FB("acc", co * 1024 + ts * 512, co * 1024 + (ts + 1) * 512),
                        psum[bk][:, :],
                        FB("acc", co * 1024 + ts * 512, co * 1024 + (ts + 1) * 512),
                        ALU.add),
                      reads=[f"P{bk}", f"acc{co}_{ts}"], writes=[f"acc{co}_{ts}"],
                      tag=f"dE{tagp}_{ts}")

        def emit_wo_part(half, part, acc_first):
            for co in range(NCH):
                sl = co % 2
                DMA("g", BB(f"wob{sl}").rearrange("p (c f) -> p c f", c=8),
                    ap["wo"][co][:, part * 8:(part + 1) * 8, :], f"wo{sl}",
                    writes=[f"wob{sl}", "ustW"], tag=f"woL{half}_{part}_{co}")
                for ts in range(2):
                    bk = 4 + (co * 2 + ts) % 2
                    for cc_ in range(8):
                        ch = part * 8 + cc_
                        E("t", lambda e, cc_=cc_, ts=ts, bk=bk, sl=sl: e.matmul(
                            psum[bk][:, :], BB(f"wob{sl}", cc_ * 128, (cc_ + 1) * 128),
                            BB("rwkv", cc_ * HALF + ts * 512, cc_ * HALF + (ts + 1) * 512),
                            start=(cc_ == 0), stop=(cc_ == 7)),
                          reads=[f"wob{sl}", "ustW", f"rk{cc_}"], writes=[f"P{bk}"],
                          tag=f"wo{half}_{part}_{co}_{ts}_{cc_}")
                    if acc_first.pop((co, ts), False):
                        E("v", lambda e, co=co, ts=ts, bk=bk: e.tensor_copy(
                            FB("acc", co * 1024 + ts * 512, co * 1024 + (ts + 1) * 512),
                            psum[bk][:, :]),
                          reads=[f"P{bk}"], writes=[f"acc{co}_{ts}"],
                          tag=f"woE{half}_{part}_{co}_{ts}")
                    else:
                        E("v", lambda e, co=co, ts=ts, bk=bk: e.tensor_tensor(
                            FB("acc", co * 1024 + ts * 512, co * 1024 + (ts + 1) * 512),
                            psum[bk][:, :],
                            FB("acc", co * 1024 + ts * 512, co * 1024 + (ts + 1) * 512),
                            ALU.add),
                          reads=[f"P{bk}", f"acc{co}_{ts}"], writes=[f"acc{co}_{ts}"],
                          tag=f"woE{half}_{part}_{co}_{ts}")

        for half in range(2):
            hc0 = half * HALF
            acc_first = {(co, ts): True for co in range(NCH) for ts in range(2)}
            DMA("g", BB("xnh").rearrange("p (c f) -> p c f", c=NCH),
                scr_xn[half], "xnh", writes=["xnh"],
                extra_deps=(last_dma("sxn"),), tag=f"xnhL{half}")
            q = [(h, p) for h in range(8) for p in (1, 2, 3)]
            done = 0
            for fj in range(NF):
                fa = fj
                emit_gu(half, f"S{half}_{fa}", ap["sg"][fa], ap["su"][fa], fj, fj,
                        False)
                want = min(len(q), (24 * (fj + 1) + NF - 1) // NF)
                while done < want:
                    h, p = q[done]; done += 1
                    emit_corr(half, h, p)
            while done < len(q):
                h, p = q[done]; done += 1
                emit_corr(half, h, p)
            emit_wo_part(half, 0, acc_first)
            for co in range(NCH):
                emit_down(half, f"SD{half}_0_{co}", ap["sd"][co, 0], co, acc_first)
            q = [(h, p) for h in range(8, 16) for p in (1, 2, 3)]
            done = 0
            for fj in range(NF):
                fa = NF + fj
                emit_gu(half, f"S{half}_{fa}", ap["sg"][fa], ap["su"][fa], fj, fj,
                        False)
                want = min(len(q), (24 * (fj + 1) + NF - 1) // NF)
                while done < want:
                    h, p = q[done]; done += 1
                    emit_corr(half, h, p)
            while done < len(q):
                h, p = q[done]; done += 1
                emit_corr(half, h, p)
            emit_wo_part(half, 1, acc_first)
            for co in range(NCH):
                emit_down(half, f"SD{half}_1_{co}", ap["sd"][co, 1], co, acc_first)
            for n in range(NEXP):
                g, jj = n // 2, n % 2
                for tt in range(8):
                    bk = 6 + tt // 4
                    E("t", lambda e, tt=tt, g=g, jj=jj, bk=bk, half=half: e.matmul(
                        psum[bk][:, (tt % 4) * 128:(tt % 4) * 128 + 128],
                        FB("oneh", g * 128, (g + 1) * 128, p0=32 * jj, p1=32 * jj + 4),
                        FB("pk", (half * 8 + tt) * 128, (half * 8 + tt + 1) * 128,
                           p0=32 * jj, p1=32 * jj + 4),
                        start=True, stop=True),
                      reads=["oneh", "w0T" if jj == 0 else "w1T"], writes=[f"P{bk}"],
                      tag=f"wbM{half}_{n}_{tt}")
                    if tt % 4 == 3:
                        E("v", lambda e, tt=tt, bk=bk: e.tensor_copy(
                            FB("wb", (tt // 4) * 512, (tt // 4) * 512 + 512),
                            psum[bk][:, :]),
                          reads=[f"P{bk}"], writes=list(WBK), tag=f"wbE{half}_{n}_{tt}")
                for fj in range(NF):
                    emit_gu(half, f"E{half}_{n}_{fj}", ap["gw"][n, fj], ap["up"][n, fj],
                            fj, fj, True)
                for co in range(NCH):
                    emit_down(half, f"ED{half}_{n}_{co}", ap["dw"][n, co], co, acc_first)
            for tt in range(8):
                sl = tt % 2
                grow = hc0 + tt * 128
                xk = XR0 if sl == 0 else XR1
                DMA("y", FB(f"xres{sl}"), ap["x"][grow:grow + 128, :], f"xr{sl}",
                    writes=list(xk), tag=f"xr{half}_{tt}")
                for q in range(4):
                    bk = q % 2
                    for u in range(4):
                        co = q * 4 + u
                        E("t", lambda e, co=co, u=u, tt=tt, bk=bk: e.transpose(
                            psum[bk][:, u * 128:(u + 1) * 128],
                            FB("acc", co * 1024 + tt * 128, co * 1024 + (tt + 1) * 128),
                            FB("idf")),
                          reads=[f"acc{co}_{tt // 4}", "idf"], writes=[f"P{bk}"],
                          tag=f"oT{half}_{tt}_{co}")
                    E("v", lambda e, q=q, bk=bk, sl=sl: e.tensor_tensor(
                        FB("outtm", q * 512, (q + 1) * 512), psum[bk][:, :],
                        FB(f"xres{sl}", q * 512, (q + 1) * 512), ALU.add),
                      reads=[f"P{bk}"] + list(xk), writes=list(OTM),
                      tag=f"oE{half}_{tt}_{q}")
                DMA("y", out[grow:grow + 128, :], FB("outtm"), "outw", "full",
                    reads=list(OTM), tag=f"ow{half}_{tt}")
        P.join()

        return _emit(nc, P, sems_engine, sems_dma)


def _emit(nc, P, sems_engine, sems_dma):
    if DEBUG_TRUNC is not None:
        P.ops = P.ops[:DEBUG_TRUNC]
    P.finalize()
    engmap = {"t": "tensor", "v": "vector", "s": "scalar", "y": "sync",
              "g": "gpsimd"}
    with nc.Block() as blk:
        for eng in ("t", "v", "s", "y", "g"):
            def make(eng):
                def emitter(E_):
                    floor = {}
                    for o in P.ops:
                        if o["eng"] != eng:
                            continue
                        for sem, val in sorted(o["waits"].items()):
                            if val <= floor.get(sem, 0):
                                continue
                            h = (sems_engine[sem[1:]] if sem.startswith("E")
                                 else sems_dma[sem][0])
                            E_.wait_ge(h, val)
                            floor[sem] = val
                        if o["emit"] is None:
                            continue
                        inst = o["emit"](E_)
                        if o["dma"] is not None:
                            name, inc, mode, idx = o["dma"]
                            inst.then_inc(sems_dma[name][0], inc)
                        elif o["inc"]:
                            inst.then_inc(sems_engine[eng], 1)
                return emitter
            getattr(blk, engmap[eng])(make(eng))
    return nc


_CACHE = {}


def kernel(**inputs):
    d = _prep(inputs)
    if "nc" not in _CACHE:
        _CACHE["nc"] = _build()
    nc = _CACHE["nc"]
    in_maps = []
    for c in range(NCORE):
        in_maps.append({
            "x": d["x_slices"][c],
            "wk": d["wk"], "wv": d["wv"], "wr": d["wr"], "wo": d["wo"],
            "gw": d["gw"], "up": d["up"], "dw": d["dw"],
            "sg": d["sg"], "su": d["su"], "sd": d["sd"], "gwt": d["gwt"],
            "tdec": d["tdec"], "tf_eff": d["tf_eff"][c],
            "evec": d["evec"][c], "mvec": d["mvec"][c],
            "ident": d["ident"], "oneh": d["oneh"],
        })
    res = run_bass_kernel_spmd(nc, in_maps, core_ids=list(range(NCORE)))
    full = np.concatenate([res.results[c]["out"] for c in range(NCORE)], axis=0)
    return full.reshape(1, 16384, 2048)


# version 14
# speedup vs baseline: 1.0689x; 1.0145x over previous
import bisect
import contextlib
import numpy as np
import concourse.bass as bass
import concourse.mybir as mybir
from concourse.bass_utils import run_bass_kernel_spmd

F32 = mybir.dt.float32
BF16 = mybir.dt.bfloat16
AF = mybir.ActivationFunctionType
ALU = mybir.AluOpType
AX = mybir.AxisListType.X

NCORE = 8
TLOC = 2048
C = 2048
NCH = 16
NHEAD = 16
NEXP = 8
NF = 11
NFS = 22
HALF = 1024
E10 = float(np.float32(np.exp(10.0)))
EM10 = float(np.float32(np.exp(-10.0)))


class Prog:
    def __init__(self):
        self.ops = []
        self.lastw = {}
        self.readers = {}
        self.dma_cnt = {}

    def op(self, eng, emit, reads=(), writes=(), dma=None, tag="", extra_deps=(),
           small=False):
        oid = len(self.ops)
        deps = set(extra_deps)
        for k in reads:
            w = self.lastw.get(k)
            if w is not None:
                deps.add(w)
            self.readers.setdefault(k, []).append(oid)
        for k in writes:
            for r in self.readers.get(k, ()):
                deps.add(r)
            w = self.lastw.get(k)
            if w is not None:
                deps.add(w)
            self.readers[k] = []
            self.lastw[k] = oid
        deps.discard(oid)
        d = None
        if dma is not None:
            name, inc, mode = dma
            idx = self.dma_cnt.get(name, 0) + 1
            self.dma_cnt[name] = idx
            d = (name, inc, mode, idx)
        self.ops.append({"id": oid, "eng": eng, "emit": emit, "deps": deps,
                         "dma": d, "inc": False, "ord": None, "tag": tag,
                         "small": small})
        return oid

    def join(self):
        lasts = {}
        for o in self.ops:
            if o["emit"] is not None:
                lasts[o["eng"]] = o["id"]
        dma_marks = dict(self.dma_cnt)
        for eng in ("t", "v", "s", "y", "g"):
            oid = len(self.ops)
            deps = set(v for k, v in lasts.items() if k != eng)
            self.ops.append({"id": oid, "eng": eng, "emit": None, "deps": deps,
                             "dma": None, "inc": False, "ord": None,
                             "tag": "join", "dma_full": dma_marks})

    def finalize(self):
        sem_positions = {}
        for o in self.ops:
            if o["dma"] is not None:
                sem_positions.setdefault(o["dma"][0], []).append(o["id"])
        for o in self.ops:
            for d in o["deps"]:
                dep = self.ops[d]
                if dep["emit"] is None:
                    continue
                if dep["dma"] is None and (dep["eng"] != o["eng"]
                                           or dep.get("small")):
                    dep["inc"] = True
        cnt = {e: 0 for e in ("t", "v", "s", "y", "g")}
        for o in self.ops:
            if o["inc"]:
                cnt[o["eng"]] += 1
                o["ord"] = cnt[o["eng"]]
        for o in self.ops:
            waits = {}
            def addw(sem, val):
                if val > waits.get(sem, -1):
                    waits[sem] = val
            for d in o["deps"]:
                dep = self.ops[d]
                if dep["emit"] is None:
                    continue
                if dep["dma"] is not None:
                    name, inc, mode, idx = dep["dma"]
                    if mode == "full":
                        pos = sem_positions[name]
                        tot = bisect.bisect_left(pos, o["id"])
                        addw(name, inc * tot)
                    else:
                        addw(name, inc * idx)
                elif dep["eng"] != o["eng"] or dep.get("small"):
                    addw("E" + dep["eng"], dep["ord"])
            if "dma_full" in o:
                for name, tot in o["dma_full"].items():
                    waits_inc = 1 if name == "cc" else 16
                    addw(name, waits_inc * tot)
            o["waits"] = waits
        return cnt


def _prep(inputs):
    d = {}
    x = np.ascontiguousarray(np.asarray(inputs["L0_attn_x"], np.float32).reshape(16384, 2048))
    d["x_slices"] = [np.ascontiguousarray(x[c * TLOC:(c + 1) * TLOC]) for c in range(NCORE)]

    def t2(w):
        return np.ascontiguousarray(np.asarray(w, np.float32).T)

    def tile4(wt, kchunks, schunks):
        a = wt.reshape(kchunks, 128, schunks, 128)
        return np.ascontiguousarray(a.transpose(2, 1, 0, 3))

    d["wk"] = tile4(t2(inputs["L0_attn_Wk"]), NCH, NHEAD)
    d["wv"] = tile4(t2(inputs["L0_attn_Wv"]), NCH, NHEAD)
    d["wr"] = tile4(t2(inputs["L0_attn_Wr"]), NCH, NHEAD)
    d["wo"] = tile4(t2(inputs["L0_attn_Wo"]), NCH, NCH)

    gw = np.asarray(inputs["L0_ffn_gate_w"], np.float32)
    up = np.asarray(inputs["L0_ffn_up_w"], np.float32)
    dw = np.asarray(inputs["L0_ffn_down_w"], np.float32)
    d["gw"] = np.ascontiguousarray(
        gw.transpose(0, 2, 1).reshape(NEXP, NCH, 128, NF, 128).transpose(0, 3, 2, 1, 4))
    d["up"] = np.ascontiguousarray(
        up.transpose(0, 2, 1).reshape(NEXP, NCH, 128, NF, 128).transpose(0, 3, 2, 1, 4))
    d["dw"] = np.ascontiguousarray(
        dw.transpose(0, 2, 1).reshape(NEXP, NF, 128, NCH, 128).transpose(0, 3, 2, 1, 4))
    sg = t2(inputs["L0_ffn_sh_gate_w"])
    su = t2(inputs["L0_ffn_sh_up_w"])
    sd = t2(inputs["L0_ffn_sh_down_w"])
    d["sg"] = np.ascontiguousarray(sg.reshape(NCH, 128, NFS, 128).transpose(2, 1, 0, 3))
    d["su"] = np.ascontiguousarray(su.reshape(NCH, 128, NFS, 128).transpose(2, 1, 0, 3))
    sd4 = np.ascontiguousarray(sd.reshape(NFS, 128, NCH, 128).transpose(2, 1, 0, 3))
    d["sd"] = np.ascontiguousarray(
        sd4.reshape(NCH, 128, 2, NF, 128).transpose(0, 2, 1, 3, 4))
    gwt = t2(inputs["L0_ffn_gate_weight"])
    d["gwt"] = np.ascontiguousarray(gwt.reshape(NCH, 128, 8).transpose(1, 0, 2))

    d["tdec"] = np.ascontiguousarray(np.asarray(inputs["L0_attn_time_decay"], np.float32))
    tf = np.ascontiguousarray(np.asarray(inputs["L0_attn_time_first"], np.float32))
    z16 = np.zeros_like(tf)
    d["tf_eff"] = [tf if c == 0 else z16 for c in range(NCORE)]

    ev, mv = [], []
    for c in range(NCORE):
        e = np.zeros((1, 8), np.float32)
        m = np.zeros((1, 8), np.float32)
        for r in range(c):
            e[0, r] = float(TLOC * (c - 1 - r))
            m[0, r] = 1.0
        ev.append(e); mv.append(m)
    d["evec"] = ev; d["mvec"] = mv
    d["ident"] = np.eye(128, dtype=np.float32)
    oneh = np.zeros((4, 512), np.float32)
    for g in range(4):
        oneh[g, g * 128:(g + 1) * 128] = 1.0
    d["oneh"] = oneh
    return d


DEBUG_STAGE = None
DEBUG_NJT = 16
DEBUG_TRUNC = None


def _build():
    nc = bass.Bass("TRN2", target_bir_lowering=False)
    ap = {}

    def din(name, shape, dt=F32):
        ap[name] = nc.dram_tensor(name, shape, dt, kind="ExternalInput").ap()
    din("x", [TLOC, C])
    din("wk", [NHEAD, 128, NCH, 128]); din("wv", [NHEAD, 128, NCH, 128])
    din("wr", [NHEAD, 128, NCH, 128]); din("wo", [NCH, 128, NCH, 128])
    din("gw", [NEXP, NF, 128, NCH, 128]); din("up", [NEXP, NF, 128, NCH, 128])
    din("dw", [NEXP, NCH, 128, NF, 128])
    din("sg", [NFS, 128, NCH, 128]); din("su", [NFS, 128, NCH, 128])
    din("sd", [NCH, 2, 128, NF, 128])
    din("gwt", [128, NCH, 8])
    din("tdec", [16, 128]); din("tf_eff", [16, 128])
    din("evec", [1, 8]); din("mvec", [1, 8])
    din("ident", [128, 128]); din("oneh", [4, 512])
    out = nc.dram_tensor("out", [TLOC, C], F32, kind="ExternalOutput").ap()

    dbg = DEBUG_STAGE
    skind = "ExternalOutput" if dbg in ("s1", "s2", "s3") else Ellipsis
    if dbg is not None:
        dbg_xn = nc.dram_tensor("dbg_xn", [128, NCH * 2048], BF16, kind="ExternalOutput").ap()
        dbg_pk = nc.dram_tensor("dbg_pk", [128, 2048], F32, kind="ExternalOutput").ap()
    if dbg in ("s3",):
        dbg_c = nc.dram_tensor("dbg_c", [128, 384], F32, kind="ExternalOutput").ap()
    scr_xn = nc.dram_tensor("scr_xn", [2, 128, NCH, HALF], BF16).ap()
    sk = dict(kind="ExternalOutput") if dbg in ("s2", "s3") else {}
    scr_num = nc.dram_tensor("scr_num", [NHEAD, 128, TLOC], F32, **sk).ap()
    scr_den = nc.dram_tensor("scr_den", [NHEAD, 128, TLOC], F32, **sk).ap()
    scr_sr = nc.dram_tensor("scr_sr", [NHEAD, 128, TLOC], F32, **sk).ap()
    scr_pow = nc.dram_tensor("scr_pow", [NHEAD, 128, TLOC], F32).ap()
    contrib = nc.dram_tensor("contrib", [32, 128], F32).ap()
    gathered = nc.dram_tensor("gathered", [256, 128], F32).ap()

    P = Prog()

    fmap = {}
    _off = [0]
    def fal(name, cols, base=None):
        o = _off[0] if base is None else base
        fmap[name] = (o, cols)
        if base is None:
            _off[0] += cols
    fal("ones", 2048); fal("iota", 1024); fal("idf", 128); fal("pk", 2048)
    fal("gwt", 128); fal("oneh", 512); fal("csm", 448)
    fal("w0all", 64); fal("w1all", 64); fal("a2t", 128)
    G_F32 = _off[0]
    fal("xtm0", 2048, G_F32); fal("xtm1", 2048, G_F32 + 2048)
    fal("xntm0", 2048, G_F32 + 4096); fal("xntm1", 2048, G_F32 + 6144)
    fal("xnf0", 128, G_F32 + 8192); fal("xnf1", 128, G_F32 + 8320)
    fal("xnfF", 2048, G_F32 + 10944)
    fal("dump", 2048, G_F32 + 8448); fal("lns", 64, G_F32 + 10496)
    fal("rsm", 128, G_F32 + 10560)
    szA1 = 12992
    fal("w0", 1024, G_F32); fal("w1", 1024, G_F32 + 1024)
    fal("wv0", 1024, G_F32 + 2048); fal("wv1", 1024, G_F32 + 3072)
    fal("db", 2048, G_F32 + 4096)
    fal("num", 2048, G_F32 + 6144); fal("den", 2048, G_F32 + 8192)
    fal("sr", 2048, G_F32 + 10240); fal("pw2", 2048, G_F32 + 12288)
    szA2 = 14336
    fal("numh", 1024, G_F32); fal("denh", 1024, G_F32 + 1024)
    fal("srh", 1024, G_F32 + 2048); fal("powrec", 1024, G_F32 + 3072)
    fal("obuf", 2048, G_F32 + 4096)
    fal("acc", 16 * 1024, G_F32 + 6144)
    fal("wb", 1024, G_F32)
    szB = 6144 + 16384
    fal("xres0", 2048, G_F32)
    fal("xres1", 2048, G_F32 + 2048)
    fal("outtm", 2048, G_F32 + 4096)
    F32_COLS = G_F32 + max(szA1, szA2, szB)

    bmap = {}
    bmap["xnbf"] = (0, NCH * 2048)
    bo = NCH * 2048
    for nm in ("wkb0", "wkb1", "wvb0", "wvb1", "wrb0", "wrb1"):
        bmap[nm] = (bo, 2048); bo += 2048
    BF_A = bo
    bmap["xnh"] = (0, NCH * HALF)
    r0 = NCH * HALF
    bmap["rwkv"] = (r0, 8 * HALF)
    r1 = r0 + 8 * HALF
    bmap["hp"] = (r1, NF * HALF)
    r2 = r1 + NF * HALF
    bmap["gst0"] = (r2, 2048); bmap["gst1"] = (r2 + 2048, 2048)
    bmap["ust0"] = (r2 + 4096, 2048)
    bmap["dst0"] = (r2 + 6144, NF * 128); bmap["dst1"] = (r2 + 6144 + NF * 128, NF * 128)
    bmap["wob0"] = (r2 + 6144 + 2 * NF * 128, 1024)
    bmap["wob1"] = (r2 + 7168 + 2 * NF * 128, 1024)
    bmap["ustW"] = (r2 + 6144 + 2 * NF * 128, 2048)
    BF_B = r2 + 8192 + 2 * NF * 128
    BF_COLS = max(BF_A, BF_B)
    bpp = F32_COLS * 4 + BF_COLS * 2
    assert bpp <= 207 * 1024, f"SBUF over budget: {bpp} B/partition"

    st = contextlib.ExitStack()
    sems_engine = {}
    sems_dma = {}
    with st:
        AF32 = st.enter_context(nc.sbuf_tensor("AF32", [128, F32_COLS], F32))
        ABF = st.enter_context(nc.sbuf_tensor("ABF", [128, BF_COLS], BF16))
        psum = [st.enter_context(nc.psum_tensor(f"pb{i}", [128, 512], F32))
                for i in range(8)]

        def FB(name, a=0, b=None, p0=0, p1=128):
            o, n = fmap[name]
            if b is None:
                b = n
            return AF32[p0:p1, o + a:o + b]

        def BB(name, a=0, b=None):
            o, n = bmap[name]
            if b is None:
                b = n
            return ABF[:, o + a:o + b]

        for e in ("t", "v", "s", "g"):
            sems_engine[e] = st.enter_context(nc.semaphore(f"se_{e}"))
        for name in ("ldc", "xt0", "xt1", "wk0", "wk1", "wv0", "wv1", "wr0", "wr1",
                     "sxn", "snum", "sden", "ssr", "cwr", "cc", "grd",
                     "xnh", "nh0", "nh1", "dh0", "dh1", "sh0", "sh1", "spow", "ph0", "ph1",
                     "wo0", "wo1", "gs0", "gs1", "us0", "us1", "dsd", "ds1",
                     "xr0", "xr1", "outw"):
            inc = 1 if name == "cc" else 16
            sems_dma[name] = (st.enter_context(nc.semaphore(f"sd_{name}")), inc)

        def E(eng, fn, reads=(), writes=(), tag="", extra_deps=(), small=False):
            return P.op(eng, fn, reads=reads, writes=writes, tag=tag,
                        extra_deps=extra_deps, small=small)

        def DMA(queue, dst, src, sem, mode="slot", reads=(), writes=(), tag="",
                extra_deps=()):
            def fn(e, dst=dst, src=src):
                return e.dma_start(out=dst, in_=src)
            return P.op(queue, fn, reads=reads, writes=writes,
                        dma=(sem, sems_dma[sem][1], mode), tag=tag,
                        extra_deps=extra_deps)

        def last_dma(name):
            for o in reversed(P.ops):
                if o["dma"] is not None and o["dma"][0] == name:
                    return o["id"]
            raise KeyError(name)

        LAM = ("csm", 0, 16); DEC = ("csm", 16, 32); FT = ("csm", 32, 48)
        A2C = ("csm", 48, 80); CAN = ("csm", 80, 96); CAD = ("csm", 96, 112)
        EVE = ("csm", 112, 120); MVE = ("csm", 120, 128)
        TDR = ("csm", 128, 256)
        NL1 = ("csm", 256, 272)
        EBB = ("csm", 272, 280); MBB = ("csm", 280, 288)
        GTMP = ("csm", 288, 304)
        GT2 = ("csm", 304, 320)
        FRT = ("csm", 320, 336); TRT = ("csm", 336, 352)
        FR2 = ("csm", 352, 368)
        CANB = ("csm", 368, 384); CADB = ("csm", 384, 400)
        DP24 = ("csm", 400, 416)

        DMA("y", FB("idf"), ap["ident"], "ldc", "full", writes=["idf"])
        DMA("y", FB("gwt"), ap["gwt"].rearrange("p c e -> p (c e)"), "ldc", "full",
            writes=["gwt"])
        DMA("y", FB("oneh", p0=0, p1=4), ap["oneh"], "ldc", "full", writes=["oneh"])
        DMA("y", FB("oneh", p0=32, p1=36), ap["oneh"], "ldc", "full", writes=["oneh"])
        DMA("y", FB(*EVE, p0=0, p1=1), ap["evec"], "ldc", "full", writes=["ev"])
        DMA("y", FB(*MVE, p0=0, p1=1), ap["mvec"], "ldc", "full", writes=["mv"])
        DMA("y", FB(*TDR, p0=0, p1=16), ap["tdec"], "ldc", "full", writes=["tdraw"])
        E("t", lambda e: e.transpose(psum[0][0:128, 0:16], FB(*TDR, p0=0, p1=16),
                                     FB("idf", 0, 16, p0=0, p1=16)),
          reads=["tdraw", "idf"], writes=["P0"], tag="tdT")
        E("v", lambda e: e.tensor_copy(FB(*LAM), psum[0][:, 0:16]),
          reads=["P0"], writes=["lamr"], small=True)
        DMA("y", FB(*TDR, p0=0, p1=16), ap["tf_eff"], "ldc", "full",
            writes=["tdraw"], tag="tfL")
        E("t", lambda e: e.transpose(psum[1][0:128, 0:16], FB(*TDR, p0=0, p1=16),
                                     FB("idf", 0, 16, p0=0, p1=16)),
          reads=["tdraw", "idf"], writes=["P1"], tag="tfT")
        E("v", lambda e: e.tensor_copy(FB(*FT), psum[1][:, 0:16]),
          reads=["P1"], writes=["ftr"], small=True)
        E("s", lambda e: e.activation(FB(*LAM), FB(*LAM), AF.Exp),
          reads=["lamr"], writes=["lam"], small=True)
        E("s", lambda e: e.activation(FB(*DEC), FB(*LAM), AF.Exp, scale=-1.0),
          reads=["lam"], writes=["dec"], small=True)
        E("s", lambda e: e.activation(FB(*FT), FB(*FT), AF.Exp),
          reads=["ftr"], writes=["ft"], small=True)
        E("v", lambda e: e.memset(FB("ones"), 1.0), writes=["ones"])
        E("v", lambda e: e.memset(FB("pk"), 0.0), writes=["logits", "w0T", "w1T"])
        E("v", lambda e: e.memset(FB("iota"), 0.0), writes=["zer"])

        if dbg == "s0":
            DMA("y", dbg_pk[:, 0:384], AF32[:, fmap["csm"][0]:fmap["csm"][0] + 384], "outw",
                "full", reads=["lam", "dec", "ft", "iota"], tag="dbgc0")
            P.join()
            return _emit(nc, P, sems_engine, sems_dma)

        def LOGITS(a=0, b=2048):
            return FB("pk", a, b, p0=64, p1=72)
        NJ = DEBUG_NJT if dbg in ("s1a", "s1z") else NCH
        E("v", lambda e: e.memset(FB("lns", 16, 32), 0.0), writes=["ssq"])
        for j in range(NJ):
            sl = j % 2
            DMA("y", FB(f"xtm{sl}"), ap["x"][j * 128:(j + 1) * 128, :], f"xt{sl}",
                writes=[f"xtm{sl}"], tag=f"xLa{j}")
            E("v", lambda e, sl=sl, j=j: e.reduce_sum(FB("lns", j, j + 1),
                                                      FB(f"xtm{sl}"), axis=AX),
              reads=[f"xtm{sl}"], writes=["msum"], small=(j == NJ - 1), tag=f"rs{j}")
            E("s", lambda e, sl=sl, j=j: e.activation(
                FB("dump"), FB(f"xtm{sl}"), AF.Square, accum_out=FB("lns", 16 + j, 17 + j)),
              reads=[f"xtm{sl}", "ssq"], writes=["dump", "ssq"],
              small=(j == NJ - 1), tag=f"sq{j}")
        E("v", lambda e: e.tensor_scalar(FB("rsm", 0, 16), FB("lns", 0, 16),
                                         1.0 / 2048, None, ALU.mult),
          reads=["msum"], writes=["m"], small=True)
        E("v", lambda e: e.tensor_tensor(FB("rsm", 16, 32), FB("rsm", 0, 16),
                                         FB("rsm", 0, 16), ALU.mult),
          reads=["m"], writes=["m2"], small=True)
        E("v", lambda e: e.tensor_scalar(FB("rsm", 32, 48), FB("rsm", 16, 32),
                                         -1.0, None, ALU.mult),
          reads=["m2"], writes=["nm2"], small=True)
        E("v", lambda e: e.tensor_scalar(FB("rsm", 48, 64), FB("lns", 16, 32),
                                         1.0 / 2048, None, ALU.mult),
          reads=["ssq"], writes=["ex2"], small=True)
        E("v", lambda e: e.tensor_tensor(FB("rsm", 64, 80), FB("rsm", 48, 64),
                                         FB("rsm", 32, 48), ALU.add),
          reads=["ex2", "nm2"], writes=["varb"], small=True)
        E("v", lambda e: e.tensor_scalar(FB("rsm", 80, 96), FB("rsm", 64, 80),
                                         1e-5, None, ALU.add),
          reads=["varb"], writes=["vare"], small=True)
        E("s", lambda e: e.activation(FB("rsm", 96, 112), FB("rsm", 80, 96), AF.Ln),
          reads=["vare"], writes=["lnv"], small=True)
        E("s", lambda e: e.activation(FB("lns", 32, 48), FB("rsm", 96, 112), AF.Exp,
                                      scale=-0.5),
          reads=["lnv"], writes=["rstd"], small=True)
        E("v", lambda e: e.tensor_scalar(FB("rsm", 112, 128), FB("rsm", 0, 16),
                                         -1.0, None, ALU.mult),
          reads=["m"], writes=["negm"], small=True)
        E("v", lambda e: e.tensor_tensor(FB("lns", 48, 64), FB("rsm", 112, 128),
                                         FB("lns", 32, 48), ALU.mult),
          reads=["negm", "rstd"], writes=["nmr"], small=True)
        for j in range(NJ):
            sl = j % 2
            DMA("y", FB(f"xtm{sl}"), ap["x"][j * 128:(j + 1) * 128, :], f"xt{sl}",
                writes=[f"xtm{sl}"], tag=f"xLb{j}")
            E("v", lambda e, sl=sl, j=j: e.tensor_scalar(
                FB(f"xntm{sl}"), FB(f"xtm{sl}"), FB("lns", 32 + j, 33 + j),
                FB("lns", 48 + j, 49 + j), ALU.mult, ALU.add),
              reads=[f"xtm{sl}", "rstd", "nmr"], writes=[f"xntm{sl}"])
            for i in range(NCH):
                bk = i % 4
                E("t", lambda e, i=i, bk=bk, sl=sl: e.transpose(
                    psum[bk][:, 0:128], FB(f"xntm{sl}", i * 128, (i + 1) * 128),
                    FB("idf")),
                  reads=[f"xntm{sl}", "idf"], writes=[f"P{bk}"], tag=f"xT{j}_{i}")
                E("s", lambda e, bk=bk, i=i: e.copy(
                    FB("xnfF", i * 128, (i + 1) * 128), psum[bk][:, 0:128]),
                  reads=[f"P{bk}"], writes=[f"xnfF{i}"])
                E("v", lambda e, i=i, j=j: e.tensor_copy(
                    BB("xnbf", i * 2048 + j * 128, i * 2048 + (j + 1) * 128),
                    FB("xnfF", i * 128, (i + 1) * 128)),
                  reads=[f"xnfF{i}"], writes=[f"xnbf{i}"])
            for i in range(NCH):
                E("t", lambda e, i=i: e.matmul(
                    psum[4][0:8, 0:128], FB("gwt", i * 8, (i + 1) * 8),
                    FB("xnfF", i * 128, (i + 1) * 128),
                    start=(i == 0), stop=(i == NCH - 1)),
                  reads=[f"xnfF{i}", "gwt"], writes=["P4"], tag=f"gate{j}_{i}")
            E("v", lambda e, j=j: e.tensor_copy(LOGITS(j * 128, (j + 1) * 128),
                                                psum[4][0:8, 0:128]),
              reads=["P4"], writes=["logits"])
        if dbg == "s1z":
            DMA("g", dbg_xn[:, 0:2048], FB("xtm0"), "outw", "full",
                reads=["xtm0"], tag="dz1")
            DMA("g", dbg_xn[:, 2048:4096], FB("xntm0"), "outw", "full",
                reads=["xntm0"], tag="dz2")
            DMA("g", dbg_xn[:, 4096:4160], FB("lns"), "outw", "full",
                reads=["m", "ssq", "varb", "rstd", "nmr"], tag="dz3")
            DMA("y", dbg_xn[:, 6144:8192], BB("xnbf", 0, 2048), "outw", "full",
                reads=["xnbf0"], tag="dz4")
            P.join()
            return _emit(nc, P, sems_engine, sems_dma)

        if dbg == "s1a":
            DMA("y", dbg_xn, BB("xnbf"), "outw", "full",
                reads=[f"xnbf{i}" for i in range(NCH)], tag="dbgxn")
            DMA("y", dbg_pk, AF32[:, fmap["pk"][0]:fmap["pk"][0] + 2048], "outw",
                "full", reads=["logits"], tag="dbgpk")
            P.join()
            return _emit(nc, P, sems_engine, sems_dma)

        for half in range(2):
            for i in range(NCH):
                DMA("y", scr_xn[half, :, i, :],
                    BB("xnbf", i * 2048 + half * HALF, i * 2048 + half * HALF + HALF),
                    "sxn", "full", reads=[f"xnbf{i}"], tag=f"sxnw{half}_{i}")

        def _dbg_dump_s1x():
            DMA("y", dbg_xn, BB("xnbf"), "outw", "full",
                reads=[f"xnbf{i}" for i in range(NCH)], tag="dbgxn")
            DMA("y", dbg_pk, AF32[:, fmap["pk"][0]:fmap["pk"][0] + 2048], "outw",
                "full", reads=["logits"], tag="dbgpk")
            P.join()

        if dbg == "s1b":
            _dbg_dump_s1x()
            return _emit(nc, P, sems_engine, sems_dma)

        def R(a, b):
            return FB("rsm", a, b)
        for j in range(NCH):
            E("t", lambda e, j=j: e.transpose(psum[5][0:128, 0:8],
                                              LOGITS(j * 128, (j + 1) * 128),
                                              FB("idf", 64, 72, p0=64, p1=72)),
              reads=["logits", "idf"], writes=["P5"], tag=f"lgT{j}")
            E("v", lambda e: e.tensor_copy(R(0, 8), psum[5][:, 0:8]),
              reads=["P5"], writes=["lgtm"], small=True)
            E("v", lambda e: e.reduce_max(R(8, 9), R(0, 8), axis=AX),
              reads=["lgtm"], writes=["mx"], small=True)
            E("v", lambda e: e.tensor_scalar(R(9, 10), R(8, 9), -1.0, None, ALU.mult),
              reads=["mx"], writes=["nmx"], small=True)
            E("v", lambda e: e.tensor_scalar(R(10, 18), R(0, 8), R(9, 10), None, ALU.add),
              reads=["lgtm", "nmx"], writes=["sh"], small=True)
            E("s", lambda e: e.activation(R(18, 26), R(10, 18), AF.Exp),
              reads=["sh"], writes=["exq"], small=True)
            E("v", lambda e: e.reduce_sum(R(26, 27), R(18, 26), axis=AX),
              reads=["exq"], writes=["ssum"], small=True)
            E("v", lambda e: e.reciprocal(R(27, 28), R(26, 27)),
              reads=["ssum"], writes=["rsum"], small=True)
            E("v", lambda e: e.tensor_scalar(R(28, 36), R(18, 26), R(27, 28), None,
                                             ALU.mult),
              reads=["exq", "rsum"], writes=["sc8"], small=True)
            e8 = R(28, 36).rearrange("p (g t) -> p g t", t=2)
            E("v", lambda e, e8=e8: e.tensor_tensor(R(36, 40), e8[:, :, 0],
                                                    e8[:, :, 1], ALU.max),
              reads=["sc8"], writes=["g4"], small=True)
            E("v", lambda e: e.reduce_max(R(40, 41), R(36, 40), axis=AX),
              reads=["g4"], writes=["gm1"], small=True)
            E("v", lambda e: e.tensor_scalar(R(41, 45), R(36, 40), R(40, 41), None,
                                             ALU.is_ge),
              reads=["g4", "gm1"], writes=["i1"], small=True)
            E("v", lambda e: e.scalar_tensor_tensor(R(45, 49), R(41, 45), -1e30,
                                                    R(36, 40), ALU.mult, ALU.add),
              reads=["i1", "g4"], writes=["gneg"], small=True)
            E("v", lambda e: e.reduce_max(R(49, 50), R(45, 49), axis=AX),
              reads=["gneg"], writes=["gm2"], small=True)
            E("v", lambda e: e.tensor_scalar(R(50, 54), R(36, 40), R(49, 50), None,
                                             ALU.is_ge),
              reads=["g4", "gm2"], writes=["gmask"], small=True)
            E("v", lambda e, e8=e8: e.tensor_tensor(R(54, 58), e8[:, :, 0],
                                                    R(50, 54), ALU.mult),
              reads=["sc8", "gmask"], writes=["m0"], small=True)
            E("v", lambda e, e8=e8: e.tensor_tensor(R(58, 62), e8[:, :, 1],
                                                    R(50, 54), ALU.mult),
              reads=["sc8", "gmask"], writes=["m1"], small=True)
            E("v", lambda e: e.reduce_max(R(62, 63), R(54, 58), axis=AX),
              reads=["m0"], writes=["t1a"], small=True)
            E("v", lambda e: e.reduce_max(R(63, 64), R(58, 62), axis=AX),
              reads=["m1"], writes=["t1b"], small=True)
            E("v", lambda e: e.tensor_tensor(R(64, 65), R(62, 63), R(63, 64), ALU.max),
              reads=["t1a", "t1b"], writes=["t1"], small=True)
            E("v", lambda e: e.tensor_scalar(R(65, 69), R(54, 58), R(64, 65), None,
                                             ALU.is_ge),
              reads=["m0", "t1"], writes=["k0a"], small=True)
            E("v", lambda e: e.scalar_tensor_tensor(R(69, 73), R(65, 69), -1e30,
                                                    R(54, 58), ALU.mult, ALU.add),
              reads=["k0a", "m0"], writes=["k0n"], small=True)
            E("v", lambda e: e.tensor_scalar(R(73, 77), R(58, 62), R(64, 65), None,
                                             ALU.is_ge),
              reads=["m1", "t1"], writes=["k1a"], small=True)
            E("v", lambda e: e.scalar_tensor_tensor(R(77, 81), R(73, 77), -1e30,
                                                    R(58, 62), ALU.mult, ALU.add),
              reads=["k1a", "m1"], writes=["k1n"], small=True)
            E("v", lambda e: e.reduce_max(R(81, 82), R(69, 73), axis=AX),
              reads=["k0n"], writes=["t2a"], small=True)
            E("v", lambda e: e.reduce_max(R(82, 83), R(77, 81), axis=AX),
              reads=["k1n"], writes=["t2b"], small=True)
            E("v", lambda e: e.tensor_tensor(R(83, 84), R(81, 82), R(82, 83), ALU.max),
              reads=["t2a", "t2b"], writes=["t2"], small=True)
            E("v", lambda e: e.tensor_scalar(R(84, 88), R(54, 58), R(83, 84), None,
                                             ALU.is_ge),
              reads=["m0", "t2"], writes=["k0"], small=True)
            E("v", lambda e: e.tensor_scalar(R(88, 92), R(58, 62), R(83, 84), None,
                                             ALU.is_ge),
              reads=["m1", "t2"], writes=["k1"], small=True)
            E("v", lambda e, j=j: e.tensor_tensor(FB("w0all", j * 4, (j + 1) * 4),
                                                  R(54, 58), R(84, 88), ALU.mult),
              reads=["m0", "k0"], writes=["w0all"], small=True)
            E("v", lambda e, j=j: e.tensor_tensor(FB("w1all", j * 4, (j + 1) * 4),
                                                  R(58, 62), R(88, 92), ALU.mult),
              reads=["m1", "k1"], writes=["w1all"], small=True)
        for j in range(NCH):
            E("t", lambda e, j=j: e.transpose(psum[6][0:4, 0:128],
                                              FB("w0all", j * 4, (j + 1) * 4),
                                              FB("idf")),
              reads=["w0all", "idf"], writes=["P6"], tag=f"w0T{j}")
            E("v", lambda e, j=j: e.tensor_copy(
                FB("pk", j * 128, (j + 1) * 128, p0=0, p1=4), psum[6][0:4, 0:128]),
              reads=["P6"], writes=["w0T"])
            E("t", lambda e, j=j: e.transpose(psum[7][0:4, 0:128],
                                              FB("w1all", j * 4, (j + 1) * 4),
                                              FB("idf")),
              reads=["w1all", "idf"], writes=["P7"], tag=f"w1T{j}")
            E("v", lambda e, j=j: e.tensor_copy(
                FB("pk", j * 128, (j + 1) * 128, p0=32, p1=36), psum[7][0:4, 0:128]),
              reads=["P7"], writes=["w1T"])

        P.join()

        def _dbg_dump_s1():
            DMA("y", dbg_xn, BB("xnbf"), "outw", "full",
                reads=[f"xnbf{i}" for i in range(NCH)], tag="dbgxn")
            DMA("y", dbg_pk, AF32[:, fmap["pk"][0]:fmap["pk"][0] + 2048], "outw",
                "full", reads=["logits", "w0T", "w1T"], tag="dbgpk")
            P.join()

        if dbg == "s1":
            _dbg_dump_s1()
            return _emit(nc, P, sems_engine, sems_dma)

        def wload(h, sl):
            for nm in ("wk", "wv", "wr"):
                DMA("g", BB(f"{nm}b{sl}").rearrange("p (c f) -> p c f", c=NCH),
                    ap[nm][h], f"{nm}{sl}", writes=[f"{nm}b{sl}"], tag=f"{nm}L{h}")
        wload(0, 0)
        wload(1, 1)
        for h in range(NHEAD):
            sl = h % 2
            E("v", lambda e, h=h: e.tensor_scalar(FB("db"), FB("ones"),
                                                  FB(*DEC)[:, h:h + 1], None, ALU.mult),
              reads=["ones", "dec"], writes=["db"], tag=f"db{h}")
            for ts in range(4):
                wsl = FB(f"w{ts // 2}", (ts % 2) * 512, (ts % 2) * 512 + 512)
                wvsl = FB(f"wv{ts // 2}", (ts % 2) * 512, (ts % 2) * 512 + 512)
                bk, bv, br = ts % 2, 2 + ts % 2, 4 + ts % 2
                for i in range(NCH):
                    E("t", lambda e, i=i, ts=ts, bk=bk, sl=sl: e.matmul(
                        psum[bk][:, :], BB(f"wkb{sl}", i * 128, (i + 1) * 128),
                        BB("xnbf", i * 2048 + ts * 512, i * 2048 + (ts + 1) * 512),
                        start=(i == 0), stop=(i == NCH - 1)),
                      reads=[f"wkb{sl}", f"xnbf{i}"], writes=[f"P{bk}"],
                      tag=f"k{h}_{ts}_{i}")
                E("s", lambda e, wsl=wsl, bk=bk: e.activation(wsl, psum[bk][:, :], AF.Exp),
                  reads=[f"P{bk}"], writes=[f"wq{ts}"], tag=f"wexp{h}_{ts}")
                E("v", lambda e, wsl=wsl: e.tensor_scalar(wsl, wsl, E10, EM10,
                                                          ALU.min, ALU.max),
                  reads=[f"wq{ts}"], writes=[f"wq{ts}"], tag=f"wclip{h}_{ts}")
                if ts == 0:
                    E("v", lambda e, h=h: e.tensor_tensor(
                        FB(*NL1)[:, 2:3], FB("w0", 0, 1), FB(*FT)[:, h:h + 1],
                        ALU.mult),
                      reads=["wq0", "ft"], writes=["fm"], small=True,
                      tag=f"fm{h}")
                    E("v", lambda e: e.tensor_copy(FB("w0", 0, 1), FB(*NL1)[:, 2:3]),
                      reads=["fm"], writes=["wq0"], small=True, tag=f"fmc{h}")
                for i in range(NCH):
                    E("t", lambda e, i=i, ts=ts, bv=bv, sl=sl: e.matmul(
                        psum[bv][:, :], BB(f"wvb{sl}", i * 128, (i + 1) * 128),
                        BB("xnbf", i * 2048 + ts * 512, i * 2048 + (ts + 1) * 512),
                        start=(i == 0), stop=(i == NCH - 1)),
                      reads=[f"wvb{sl}", f"xnbf{i}"], writes=[f"P{bv}"],
                      tag=f"v{h}_{ts}_{i}")
                E("v", lambda e, wsl=wsl, wvsl=wvsl, bv=bv: e.tensor_tensor(
                    wvsl, psum[bv][:, :], wsl, ALU.mult),
                  reads=[f"P{bv}", f"wq{ts}"], writes=[f"wvq{ts}"], tag=f"wv{h}_{ts}")
                for i in range(NCH):
                    E("t", lambda e, i=i, ts=ts, br=br, sl=sl: e.matmul(
                        psum[br][:, :], BB(f"wrb{sl}", i * 128, (i + 1) * 128),
                        BB("xnbf", i * 2048 + ts * 512, i * 2048 + (ts + 1) * 512),
                        start=(i == 0), stop=(i == NCH - 1)),
                      reads=[f"wrb{sl}", f"xnbf{i}"], writes=[f"P{br}"],
                      tag=f"r{h}_{ts}_{i}")
                E("s", lambda e, ts=ts, br=br: e.activation(
                    FB("sr", ts * 512, (ts + 1) * 512), psum[br][:, :], AF.Tanh,
                    scale=0.5),
                  reads=[f"P{br}"], writes=["srq"], tag=f"tanh{h}_{ts}")
            if h + 2 < NHEAD:
                wload(h + 2, (h + 2) % 2)
            E("v", lambda e: e.tensor_tensor_scan(
                FB("num", 0, 1024), FB("db", 0, 1024), FB("w0" if False else "wv0"),
                0.0, ALU.mult, ALU.add),
              reads=["db", "wvq0", "wvq1"], writes=["numq0"], small=True,
              tag=f"scN0_{h}")
            E("v", lambda e: e.tensor_tensor_scan(
                FB("num", 1024, 2048), FB("db", 1024, 2048), FB("wv1"),
                FB("num", 1023, 1024), ALU.mult, ALU.add),
              reads=["db", "wvq2", "wvq3", "numq0"], writes=["numq1"], small=True,
              tag=f"scN1_{h}")
            E("v", lambda e: e.tensor_tensor_scan(
                FB("den", 0, 1024), FB("db", 0, 1024), FB("w0"),
                0.0, ALU.mult, ALU.add),
              reads=["db", "wq0", "wq1"], writes=["denq0"], small=True,
              tag=f"scD0_{h}")
            E("v", lambda e: e.tensor_tensor_scan(
                FB("den", 1024, 2048), FB("db", 1024, 2048), FB("w1"),
                FB("den", 1023, 1024), ALU.mult, ALU.add),
              reads=["db", "wq2", "wq3", "denq0"], writes=["denq1"], small=True,
              tag=f"scD1_{h}")
            E("v", lambda e, h=h: e.tensor_copy(FB(*A2C)[:, h:h + 1],
                                                FB("num", 2047, 2048)),
              reads=["numq1"], writes=["A2"], small=True, tag=f"An{h}")
            E("v", lambda e, h=h: e.tensor_copy(FB(*A2C)[:, 16 + h:17 + h],
                                                FB("den", 2047, 2048)),
              reads=["denq1"], writes=["A2"], small=True, tag=f"Ad{h}")
            E("v", lambda e: e.tensor_scalar(FB("den"), FB("den"), 1e-6, None,
                                             ALU.add),
              reads=["denq0", "denq1"], writes=["denq0", "denq1"], tag=f"depsA{h}")
            E("v", lambda e: e.tensor_tensor_scan(
                FB("pw2", 0, 1024), FB("db", 0, 1024), FB("iota"),
                1.0, ALU.mult, ALU.add),
              reads=["db", "zer"], writes=["pwq0"], small=True, tag=f"scP0_{h}")
            E("v", lambda e: e.tensor_tensor_scan(
                FB("pw2", 1024, 2048), FB("db", 1024, 2048), FB("iota"),
                FB("pw2", 1023, 1024), ALU.mult, ALU.add),
              reads=["db", "zer", "pwq0"], writes=["pwq1"], small=True,
              tag=f"scP1_{h}")
            DMA("y", scr_pow[h], FB("pw2"), "spow", "full",
                reads=["pwq0", "pwq1"], tag=f"spow{h}")
            DMA("y", scr_num[h], FB("num"), "snum", "full",
                reads=["numq0", "numq1"], tag=f"snum{h}")
            DMA("y", scr_den[h], FB("den"), "sden", "full",
                reads=["denq0", "denq1"], tag=f"sden{h}")
            DMA("y", scr_sr[h], FB("sr"), "ssr", "full", reads=["srq"], tag=f"ssr{h}")

        P.join()

        E("t", lambda e: e.transpose(psum[6][0:32, 0:128], FB(*A2C), FB("idf")),
          reads=["A2", "idf"], writes=["P6"], tag="A2T")
        E("v", lambda e: e.tensor_copy(FB("a2t", 0, 128, p0=0, p1=32),
                                       psum[6][0:32, 0:128]),
          reads=["P6"], writes=["A2T"], small=True, tag="A2Tc")
        DMA("y", contrib, FB("a2t", 0, 128, p0=0, p1=32), "cwr", "full",
            reads=["A2T"], writes=["contribd"], tag="cwr")
        def ccfn(e):
            return e.collective_compute(
                "AllGather", ALU.bypass, replica_groups=[list(range(NCORE))],
                ins=[contrib.opt()], outs=[gathered.opt()])
        P.op("g", ccfn, reads=["contribd"], writes=["gatheredd"],
             dma=("cc", 1, "slot"), tag="allgather")
        E("t", lambda e: e.matmul(psum[7][:, 0:8], FB("ones", 0, 128, p0=0, p1=1),
                                  FB(*EVE, p0=0, p1=1), start=True, stop=True),
          reads=["ones", "ev"], writes=["P7"], tag="ebB")
        E("v", lambda e: e.tensor_copy(FB(*EBB), psum[7][:, 0:8]),
          reads=["P7"], writes=["ebb"], small=True, tag="ebc")
        E("t", lambda e: e.matmul(psum[7][:, 0:8], FB("ones", 0, 128, p0=0, p1=1),
                                  FB(*MVE, p0=0, p1=1), start=True, stop=True),
          reads=["ones", "mv"], writes=["P7"], tag="mbB")
        E("v", lambda e: e.tensor_copy(FB(*MBB), psum[7][:, 0:8]),
          reads=["P7"], writes=["mbb"], small=True, tag="mbc")
        E("v", lambda e: e.memset(FB(*CAN), 0.0), writes=["can0"], small=True)
        E("v", lambda e: e.memset(FB(*CAD), 0.0), writes=["cad0"], small=True)
        for r in range(NCORE - 1):
            pa = r % 2
            CNa, CNb = (CAN, CANB) if pa == 0 else (CANB, CAN)
            CDa, CDb = (CAD, CADB) if pa == 0 else (CADB, CAD)
            ka, kb = (f"can{r}", f"can{r + 1}"), None
            DMA("y", FB(*TDR, p0=0, p1=16), gathered[r * 32:r * 32 + 16, :],
                "grd", "full", reads=["gatheredd"], writes=["tdraw"], tag=f"gn{r}")
            E("t", lambda e: e.transpose(psum[6][0:128, 0:16], FB(*TDR, p0=0, p1=16),
                                         FB("idf", 0, 16, p0=0, p1=16)),
              reads=["tdraw", "idf"], writes=["P6"], tag=f"gnT{r}")
            E("v", lambda e: e.tensor_copy(FB(*GTMP), psum[6][:, 0:16]),
              reads=["P6"], writes=["anr"], small=True, tag=f"gnc{r}")
            DMA("y", FB(*TDR, p0=0, p1=16), gathered[r * 32 + 16:r * 32 + 32, :],
                "grd", "full", reads=["gatheredd"], writes=["tdraw"], tag=f"gd{r}")
            E("t", lambda e: e.transpose(psum[6][0:128, 0:16], FB(*TDR, p0=0, p1=16),
                                         FB("idf", 0, 16, p0=0, p1=16)),
              reads=["tdraw", "idf"], writes=["P6"], tag=f"gdT{r}")
            E("v", lambda e: e.tensor_copy(FB(*GT2), psum[6][:, 0:16]),
              reads=["P6"], writes=["adr"], small=True, tag=f"gdc{r}")
            E("v", lambda e, r=r: e.tensor_scalar(FB(*FRT), FB(*LAM),
                                                  FB(*EBB)[:, r:r + 1], None, ALU.mult),
              reads=["lam", "ebb"], writes=["fr"], small=True, tag=f"f1_{r}")
            E("s", lambda e: e.activation(FB(*FR2), FB(*FRT), AF.Exp, scale=-1.0),
              reads=["fr"], writes=["fr2"], small=True, tag=f"f2_{r}")
            E("v", lambda e, r=r: e.tensor_scalar(FB(*FRT), FB(*FR2),
                                                  FB(*MBB)[:, r:r + 1], None, ALU.mult),
              reads=["fr2", "mbb"], writes=["fr"], small=True, tag=f"f3_{r}")
            E("v", lambda e: e.tensor_tensor(FB(*TRT), FB(*FRT), FB(*GTMP), ALU.mult),
              reads=["fr", "anr"], writes=["tnr"], small=True, tag=f"f4_{r}")
            E("v", lambda e, CNa=CNa, CNb=CNb: e.tensor_tensor(
                FB(*CNb), FB(*CNa), FB(*TRT), ALU.add),
              reads=["tnr", f"can{r}"], writes=[f"can{r + 1}"], small=True,
              tag=f"f5_{r}")
            E("v", lambda e: e.tensor_tensor(FB(*TRT), FB(*FRT), FB(*GT2), ALU.mult),
              reads=["fr", "adr"], writes=["tdr"], small=True, tag=f"f6_{r}")
            E("v", lambda e, CDa=CDa, CDb=CDb: e.tensor_tensor(
                FB(*CDb), FB(*CDa), FB(*TRT), ALU.add),
              reads=["tdr", f"cad{r}"], writes=[f"cad{r + 1}"], small=True,
              tag=f"f7_{r}")
        if (NCORE - 1) % 2 == 1:
            E("v", lambda e: e.tensor_copy(FB(*CAN), FB(*CANB)),
              reads=[f"can{NCORE - 1}"], writes=["can"], small=True, tag="canf")
            E("v", lambda e: e.tensor_copy(FB(*CAD), FB(*CADB)),
              reads=[f"cad{NCORE - 1}"], writes=["cad"], small=True, tag="cadf")
        else:
            E("v", lambda e: e.tensor_copy(FB(*CANB), FB(*CAN)),
              reads=[f"can{NCORE - 1}"], writes=["can"], small=True, tag="canf")
            E("v", lambda e: e.tensor_copy(FB(*CADB), FB(*CAD)),
              reads=[f"cad{NCORE - 1}"], writes=["cad"], small=True, tag="cadf")

        if dbg == "s2":
            _dbg_dump_s1()
            return _emit(nc, P, sems_engine, sems_dma)
        if dbg == "s3":
            _dbg_dump_s1()
            DMA("y", dbg_c, AF32[:, fmap["csm"][0]:fmap["csm"][0] + 384], "outw",
                "full", reads=["can", "cad", "lam", "dec", "A2"], tag="dbgc")
            P.join()
            return _emit(nc, P, sems_engine, sems_dma)

        P.join()

        WBK = ("wb", "numh")
        PWB = lambda: FB("obuf", 1024, 2048)
        XR0 = ("xres0", "numh", "denh")
        XR1 = ("xres1", "srh", "pow")
        OTM = ("outtm", "gsil", "pwb")

        def emit_corr(half, h, piece=0):
            hc0 = half * HALF
            slot = h % 8
            sl = h % 2
            if piece in (0, 1):
                _corr_p1(half, h, hc0, sl)
            if piece in (0, 2):
                _corr_p2a(half, h)
            if piece in (0, 3):
                _corr_p2b(half, h)
            if piece in (0, 4):
                _corr_p3(half, h, slot)

        def _corr_p1(half, h, hc0, sl):
            DMA("y", FB("numh"), scr_num[h][:, hc0:hc0 + HALF], f"nh{sl}",
                writes=["numh"], extra_deps=(last_dma("snum"),), tag=f"nhL{half}_{h}")
            DMA("y", FB("denh"), scr_den[h][:, hc0:hc0 + HALF], f"dh{sl}",
                writes=["denh"], extra_deps=(last_dma("sden"),), tag=f"dhL{half}_{h}")
            DMA("y", FB("srh"), scr_sr[h][:, hc0:hc0 + HALF], f"sh{sl}",
                writes=["srh"], extra_deps=(last_dma("ssr"),), tag=f"shL{half}_{h}")
            DMA("y", FB("powrec"), scr_pow[h][:, hc0:hc0 + HALF], f"ph{sl}",
                writes=["pow"], extra_deps=(last_dma("spow"),),
                tag=f"phL{half}_{h}")
            E("v", lambda e, h=h: e.scalar_tensor_tensor(
                PWB(), FB("powrec"), FB(*CAN)[:, h:h + 1], FB("numh"),
                ALU.mult, ALU.add),
              reads=["pow", "can", "numh"], writes=["pwb"], tag=f"cn{half}_{h}")
            E("v", lambda e, h=h: e.scalar_tensor_tensor(
                FB("numh"), FB("powrec"), FB(*CAD)[:, h:h + 1], FB("denh"),
                ALU.mult, ALU.add),
              reads=["pow", "cad", "denh"], writes=["numh"], tag=f"cd{half}_{h}")

        def _corr_p2a(half, h):
            E("v", lambda e: e.reciprocal(FB("denh", 0, 512), FB("numh", 0, 512)),
              reads=["numh"], writes=["denh"], tag=f"recA{half}_{h}")

        def _corr_p2b(half, h):
            E("v", lambda e: e.reciprocal(FB("denh", 512, 1024), FB("numh", 512, 1024)),
              reads=["numh"], writes=["denh"], tag=f"recB{half}_{h}")
            E("v", lambda e: e.tensor_tensor(FB("numh"), PWB(), FB("denh"),
                                             ALU.mult),
              reads=["pwb", "denh"], writes=["numh"], tag=f"wkv{half}_{h}")

        def _corr_p3(half, h, slot):
            E("g", lambda e: e.tensor_scalar(FB("powrec"), FB("srh"), 1.0, 0.5,
                                             ALU.add, ALU.mult),
              reads=["srh", "pow"], writes=["pow"], tag=f"sig{half}_{h}")
            E("v", lambda e, slot=slot: e.tensor_tensor(
                BB("rwkv", slot * HALF, (slot + 1) * HALF), FB("numh"), FB("powrec"),
                ALU.mult),
              reads=["numh", "pow"], writes=[f"rk{slot}"], tag=f"rk{half}_{h}")

        def emit_gu(half, tagp, gsrc, usrc, fj, hpj, wbmul):
            sl = fj % 2
            if wbmul and fj % 2 == 1:
                ubuf, usem, ukeys = "ustW", "us1", ["ustW", "wob0", "wob1"]
            else:
                ubuf, usem, ukeys = "ust0", "us0", ["ust0"]
            DMA("g", BB(f"gst{sl}").rearrange("p (c f) -> p c f", c=NCH),
                gsrc, f"gs{sl}", writes=[f"gst{sl}"], tag=f"gL{tagp}")
            DMA("g", BB(ubuf).rearrange("p (c f) -> p c f", c=NCH),
                usrc, usem, writes=ukeys, tag=f"uL{tagp}")
            bg = 2 * (fj % 2)
            bu = 6 - 2 * (fj % 2)
            for ts in range(2):
                for i in range(NCH):
                    E("t", lambda e, i=i, ts=ts, sl=sl, bg=bg: e.matmul(
                        psum[bg + ts][:, :], BB(f"gst{sl}", i * 128, (i + 1) * 128),
                        BB("xnh", i * HALF + ts * 512, i * HALF + (ts + 1) * 512),
                        start=(i == 0), stop=(i == NCH - 1)),
                      reads=[f"gst{sl}", "xnh"], writes=[f"P{bg + ts}"],
                      tag=f"g{tagp}_{ts}_{i}")
                E("s", lambda e, ts=ts, bg=bg: e.activation(
                    FB("obuf", ts * 512, (ts + 1) * 512), psum[bg + ts][:, :], AF.Silu),
                  reads=[f"P{bg + ts}"], writes=["gsil"], tag=f"sil{tagp}_{ts}")
                for i in range(NCH):
                    E("t", lambda e, i=i, ts=ts, ubuf=ubuf, bu=bu: e.matmul(
                        psum[bu + ts][:, :], BB(ubuf, i * 128, (i + 1) * 128),
                        BB("xnh", i * HALF + ts * 512, i * HALF + (ts + 1) * 512),
                        start=(i == 0), stop=(i == NCH - 1)),
                      reads=list(ukeys) + ["xnh"], writes=[f"P{bu + ts}"],
                      tag=f"u{tagp}_{ts}_{i}")
                if wbmul:
                    E("v", lambda e, ts=ts, bu=bu: e.tensor_tensor(
                        FB("obuf", ts * 512, (ts + 1) * 512), psum[bu + ts][:, :],
                        FB("obuf", ts * 512, (ts + 1) * 512), ALU.mult),
                      reads=[f"P{bu + ts}", "gsil"], writes=["gsil"],
                      tag=f"gu{tagp}_{ts}")
                else:
                    E("v", lambda e, ts=ts, hpj=hpj, bu=bu: e.tensor_tensor(
                        BB("hp", hpj * HALF + ts * 512, hpj * HALF + (ts + 1) * 512),
                        psum[bu + ts][:, :], FB("obuf", ts * 512, (ts + 1) * 512),
                        ALU.mult),
                      reads=[f"P{bu + ts}", "gsil"], writes=[f"hp{hpj}"],
                      tag=f"gu{tagp}_{ts}")
            if wbmul:
                E("v", lambda e, hpj=hpj: e.tensor_tensor(
                    BB("hp", hpj * HALF, (hpj + 1) * HALF), FB("obuf", 0, 1024),
                    FB("wb", 0, 1024), ALU.mult),
                  reads=["gsil"] + list(WBK), writes=[f"hp{hpj}"], tag=f"hp{tagp}")

        def emit_down(half, tagp, dsrc, co, acc_first):
            sl = co % 2
            DMA("g", BB(f"dst{sl}").rearrange("p (c f) -> p c f", c=NF),
                dsrc, "dsd" if sl == 0 else "ds1", writes=[f"dst{sl}"],
                tag=f"dL{tagp}")
            for ts in range(2):
                bk = 6 + ts
                for fj in range(NF):
                    E("t", lambda e, fj=fj, ts=ts, bk=bk, sl=sl: e.matmul(
                        psum[bk][:, :], BB(f"dst{sl}", fj * 128, (fj + 1) * 128),
                        BB("hp", fj * HALF + ts * 512, fj * HALF + (ts + 1) * 512),
                        start=(fj == 0), stop=(fj == NF - 1)),
                      reads=[f"dst{sl}", f"hp{fj}"], writes=[f"P{bk}"],
                      tag=f"d{tagp}_{ts}_{fj}")
                if acc_first.pop((co, ts), False):
                    E("v", lambda e, co=co, ts=ts, bk=bk: e.tensor_copy(
                        FB("acc", co * 1024 + ts * 512, co * 1024 + (ts + 1) * 512),
                        psum[bk][:, :]),
                      reads=[f"P{bk}"], writes=[f"acc{co}_{ts}"], tag=f"dE{tagp}_{ts}")
                else:
                    E("v", lambda e, co=co, ts=ts, bk=bk: e.tensor_tensor(
                        FB("acc", co * 1024 + ts * 512, co * 1024 + (ts + 1) * 512),
                        psum[bk][:, :],
                        FB("acc", co * 1024 + ts * 512, co * 1024 + (ts + 1) * 512),
                        ALU.add),
                      reads=[f"P{bk}", f"acc{co}_{ts}"], writes=[f"acc{co}_{ts}"],
                      tag=f"dE{tagp}_{ts}")

        def emit_wo_part(half, part, acc_first):
            for co in range(NCH):
                sl = co % 2
                DMA("g", BB(f"wob{sl}").rearrange("p (c f) -> p c f", c=8),
                    ap["wo"][co][:, part * 8:(part + 1) * 8, :], f"wo{sl}",
                    writes=[f"wob{sl}", "ustW"], tag=f"woL{half}_{part}_{co}")
                for ts in range(2):
                    bk = 4 + (co * 2 + ts) % 2
                    for cc_ in range(8):
                        ch = part * 8 + cc_
                        E("t", lambda e, cc_=cc_, ts=ts, bk=bk, sl=sl: e.matmul(
                            psum[bk][:, :], BB(f"wob{sl}", cc_ * 128, (cc_ + 1) * 128),
                            BB("rwkv", cc_ * HALF + ts * 512, cc_ * HALF + (ts + 1) * 512),
                            start=(cc_ == 0), stop=(cc_ == 7)),
                          reads=[f"wob{sl}", "ustW", f"rk{cc_}"], writes=[f"P{bk}"],
                          tag=f"wo{half}_{part}_{co}_{ts}_{cc_}")
                    if acc_first.pop((co, ts), False):
                        E("v", lambda e, co=co, ts=ts, bk=bk: e.tensor_copy(
                            FB("acc", co * 1024 + ts * 512, co * 1024 + (ts + 1) * 512),
                            psum[bk][:, :]),
                          reads=[f"P{bk}"], writes=[f"acc{co}_{ts}"],
                          tag=f"woE{half}_{part}_{co}_{ts}")
                    else:
                        E("v", lambda e, co=co, ts=ts, bk=bk: e.tensor_tensor(
                            FB("acc", co * 1024 + ts * 512, co * 1024 + (ts + 1) * 512),
                            psum[bk][:, :],
                            FB("acc", co * 1024 + ts * 512, co * 1024 + (ts + 1) * 512),
                            ALU.add),
                          reads=[f"P{bk}", f"acc{co}_{ts}"], writes=[f"acc{co}_{ts}"],
                          tag=f"woE{half}_{part}_{co}_{ts}")

        for half in range(2):
            hc0 = half * HALF
            acc_first = {(co, ts): True for co in range(NCH) for ts in range(2)}
            DMA("g", BB("xnh").rearrange("p (c f) -> p c f", c=NCH),
                scr_xn[half], "xnh", writes=["xnh"],
                extra_deps=(last_dma("sxn"),), tag=f"xnhL{half}")
            q = [(h, p) for h in range(8) for p in (1, 2, 3, 4)]
            done = 0
            for fj in range(NF):
                fa = fj
                emit_gu(half, f"S{half}_{fa}", ap["sg"][fa], ap["su"][fa], fj, fj,
                        False)
                want = min(len(q), (32 * (fj + 1) + NF - 1) // NF)
                while done < want:
                    h, p = q[done]; done += 1
                    emit_corr(half, h, p)
            while done < len(q):
                h, p = q[done]; done += 1
                emit_corr(half, h, p)
            emit_wo_part(half, 0, acc_first)
            for co in range(NCH):
                emit_down(half, f"SD{half}_0_{co}", ap["sd"][co, 0], co, acc_first)
            q = [(h, p) for h in range(8, 16) for p in (1, 2, 3, 4)]
            done = 0
            for fj in range(NF):
                fa = NF + fj
                emit_gu(half, f"S{half}_{fa}", ap["sg"][fa], ap["su"][fa], fj, fj,
                        False)
                want = min(len(q), (32 * (fj + 1) + NF - 1) // NF)
                while done < want:
                    h, p = q[done]; done += 1
                    emit_corr(half, h, p)
            while done < len(q):
                h, p = q[done]; done += 1
                emit_corr(half, h, p)
            emit_wo_part(half, 1, acc_first)
            for co in range(NCH):
                emit_down(half, f"SD{half}_1_{co}", ap["sd"][co, 1], co, acc_first)
            for n in range(NEXP):
                g, jj = n // 2, n % 2
                for tt in range(8):
                    bk = 6 + tt // 4
                    E("t", lambda e, tt=tt, g=g, jj=jj, bk=bk, half=half: e.matmul(
                        psum[bk][:, (tt % 4) * 128:(tt % 4) * 128 + 128],
                        FB("oneh", g * 128, (g + 1) * 128, p0=32 * jj, p1=32 * jj + 4),
                        FB("pk", (half * 8 + tt) * 128, (half * 8 + tt + 1) * 128,
                           p0=32 * jj, p1=32 * jj + 4),
                        start=True, stop=True),
                      reads=["oneh", "w0T" if jj == 0 else "w1T"], writes=[f"P{bk}"],
                      tag=f"wbM{half}_{n}_{tt}")
                    if tt % 4 == 3:
                        E("v", lambda e, tt=tt, bk=bk: e.tensor_copy(
                            FB("wb", (tt // 4) * 512, (tt // 4) * 512 + 512),
                            psum[bk][:, :]),
                          reads=[f"P{bk}"], writes=list(WBK), tag=f"wbE{half}_{n}_{tt}")
                for fj in range(NF):
                    emit_gu(half, f"E{half}_{n}_{fj}", ap["gw"][n, fj], ap["up"][n, fj],
                            fj, fj, True)
                for co in range(NCH):
                    emit_down(half, f"ED{half}_{n}_{co}", ap["dw"][n, co], co, acc_first)
            for tt in range(8):
                sl = tt % 2
                grow = hc0 + tt * 128
                xk = XR0 if sl == 0 else XR1
                DMA("y", FB(f"xres{sl}"), ap["x"][grow:grow + 128, :], f"xr{sl}",
                    writes=list(xk), tag=f"xr{half}_{tt}")
                for q in range(4):
                    bk = q % 2
                    for u in range(4):
                        co = q * 4 + u
                        E("t", lambda e, co=co, u=u, tt=tt, bk=bk: e.transpose(
                            psum[bk][:, u * 128:(u + 1) * 128],
                            FB("acc", co * 1024 + tt * 128, co * 1024 + (tt + 1) * 128),
                            FB("idf")),
                          reads=[f"acc{co}_{tt // 4}", "idf"], writes=[f"P{bk}"],
                          tag=f"oT{half}_{tt}_{co}")
                    E("v", lambda e, q=q, bk=bk, sl=sl: e.tensor_tensor(
                        FB("outtm", q * 512, (q + 1) * 512), psum[bk][:, :],
                        FB(f"xres{sl}", q * 512, (q + 1) * 512), ALU.add),
                      reads=[f"P{bk}"] + list(xk), writes=list(OTM),
                      tag=f"oE{half}_{tt}_{q}")
                DMA("y", out[grow:grow + 128, :], FB("outtm"), "outw", "full",
                    reads=list(OTM), tag=f"ow{half}_{tt}")
        P.join()

        return _emit(nc, P, sems_engine, sems_dma)


def _emit(nc, P, sems_engine, sems_dma):
    if DEBUG_TRUNC is not None:
        P.ops = P.ops[:DEBUG_TRUNC]
    P.finalize()
    engmap = {"t": "tensor", "v": "vector", "s": "scalar", "y": "sync",
              "g": "gpsimd"}
    with nc.Block() as blk:
        for eng in ("t", "v", "s", "y", "g"):
            def make(eng):
                def emitter(E_):
                    floor = {}
                    for o in P.ops:
                        if o["eng"] != eng:
                            continue
                        for sem, val in sorted(o["waits"].items()):
                            if val <= floor.get(sem, 0):
                                continue
                            h = (sems_engine[sem[1:]] if sem.startswith("E")
                                 else sems_dma[sem][0])
                            E_.wait_ge(h, val)
                            floor[sem] = val
                        if o["emit"] is None:
                            continue
                        inst = o["emit"](E_)
                        if o["dma"] is not None:
                            name, inc, mode, idx = o["dma"]
                            inst.then_inc(sems_dma[name][0], inc)
                        elif o["inc"]:
                            inst.then_inc(sems_engine[eng], 1)
                return emitter
            getattr(blk, engmap[eng])(make(eng))
    return nc


_CACHE = {}


def kernel(**inputs):
    d = _prep(inputs)
    if "nc" not in _CACHE:
        _CACHE["nc"] = _build()
    nc = _CACHE["nc"]
    in_maps = []
    for c in range(NCORE):
        in_maps.append({
            "x": d["x_slices"][c],
            "wk": d["wk"], "wv": d["wv"], "wr": d["wr"], "wo": d["wo"],
            "gw": d["gw"], "up": d["up"], "dw": d["dw"],
            "sg": d["sg"], "su": d["su"], "sd": d["sd"], "gwt": d["gwt"],
            "tdec": d["tdec"], "tf_eff": d["tf_eff"][c],
            "evec": d["evec"][c], "mvec": d["mvec"][c],
            "ident": d["ident"], "oneh": d["oneh"],
        })
    res = run_bass_kernel_spmd(nc, in_maps, core_ids=list(range(NCORE)))
    full = np.concatenate([res.results[c]["out"] for c in range(NCORE)], axis=0)
    return full.reshape(1, 16384, 2048)
